# version 22
# baseline (speedup 1.0000x reference)
import numpy as np
import ml_dtypes
from contextlib import ExitStack

import concourse.bass as bass
import concourse.mybir as mybir
from concourse.bass_utils import run_bass_kernel_spmd

F32 = mybir.dt.float32
BF16 = mybir.dt.bfloat16
U32 = mybir.dt.uint32
AF = mybir.ActivationFunctionType
ALU = mybir.AluOpType
AX = mybir.AxisListType
bf = ml_dtypes.bfloat16

B, N, D = 4, 4096, 1024
HEADS, DH = 4, 256
FF = 4096
TOK = 2048
NT = TOK // 128
NTF = N // 128
TB = 512
NTB = TOK // TB
EPS_LN = 1e-5
EPS_RMS = 1e-6
SCALE_QK = float(DH ** (-0.25))

DEBUG_STAGE = None

ENGINES = ("sync", "gpsimd", "tensor", "vector", "scalar")


class Prog:

    def __init__(self, nc, es):
        self.nc = nc
        self.sems = {}
        self.cnt = {}
        for name in ("DW", "DX", "DS", "DT", "DC", "T", "V", "S", "DO"):
            self.sems[name] = es.enter_context(nc.semaphore(name))
            self.cnt[name] = 0
        self.items = {e: [] for e in ENGINES}

    def op(self, eng, builder, waits=(), inc=None, dma=False):
        n = 16 if dma else 1
        fixed = []
        for (s, v) in waits:
            if s in ("DW", "DX", "DS", "DT", "DC", "DO") and 0 < v:
                v = self.cnt[s]
            fixed.append((s, v))
        if dma and inc is not None and self.cnt[inc] > 0 and inc != "DC":
            fixed.append((inc, self.cnt[inc]))
        val = None
        if inc is not None:
            self.cnt[inc] += n
            val = self.cnt[inc]
        self.items[eng].append((builder, tuple(fixed), inc, n))
        return val

    def c(self, sem):
        return self.cnt[sem]

    def flush_block(self):
        items = self.items
        self.items = {e: [] for e in ENGINES}
        sems = self.sems
        with self.nc.Block() as blk:
            def mk(ename):
                def body(eng):
                    last_wait = {}
                    for builder, waits, inc, n in items[ename]:
                        for (s, v) in waits:
                            if v > 0 and last_wait.get(s, -1) < v:
                                eng.wait_ge(sems[s], v)
                                last_wait[s] = v
                        inst = builder(eng)
                        if inc is not None:
                            inst.then_inc(sems[inc], n)
                return body
            for ename in ENGINES:
                if items[ename]:
                    getattr(blk, ename)(mk(ename))


def build(debug_stage=None):
    nc = bass.Bass("TRN2", target_bir_lowering=False)

    dr = {}
    def din(name, shape, dt=F32):
        dr[name] = nc.dram_tensor(name, list(shape), dt, kind="ExternalInput")
        return dr[name]

    x_core = din("x_core", (N, D))
    wq_h = din("wq_h", (HEADS, 128, 8 * DH))
    wk_h = din("wk_h", (HEADS, 128, 8 * DH))
    wv_h = din("wv_h", (HEADS, 128, 8 * DH))
    wo_img = din("wo_img", (128, 8 * 1024))
    wff1_img = din("wff1_img", (128, 8 * FF))
    wff2_img = din("wff2_img", (128, 32 * 1024))
    wd_img = din("wd_img", (128, 8 * 1024))
    w1t_img = din("w1t_img", (128, 8 * FF))
    w3t_img = din("w3t_img", (128, 8 * FF))
    w2t_img = din("w2t_img", (128, 32 * 1024))
    gate_img = din("gate_img", (128, 8 * 8))
    a1t_img = din("a1t_img", (128, 8 * 128))
    a3t_img = din("a3t_img", (128, 8 * 128))
    b1t_img = din("b1t_img", (128, FF))
    b3t_img = din("b3t_img", (128, FF))
    a2t_img = din("a2t_img", (128, 32 * 256))
    b2ta_img = din("b2ta_img", (128, 1024))
    b2tb_img = din("b2tb_img", (128, 1024))
    actstream = din("actstream", (128, 16 * 5120))
    actbf = nc.dram_tensor("actbf", [128, 16 * 5120], BF16)
    w2bf = nc.dram_tensor("w2bf", [128, 32 * 1024], BF16)
    wff1bf = nc.dram_tensor("wff1bf", [128, 8 * FF], BF16)
    wff2bf = nc.dram_tensor("wff2bf", [128, 32 * 1024], BF16)
    ln1g_t = din("ln1g_t", (128, D))
    ln1b_t = din("ln1b_t", (128, D))
    ln2g_t = din("ln2g_t", (128, D))
    ln2b_t = din("ln2b_t", (128, D))
    bo_t = din("bo_t", (128, D))
    bff2_t = din("bff2_t", (128, D))
    bd_t = din("bd_t", (128, D))
    nw_t = din("nw_t", (128, D))
    bff1_fm_d = din("bff1_fm", (128, 32))
    ident_bf_d = din("ident_bf", (128, 128), BF16)
    ident_f_d = din("ident_f", (128, 128), F32)
    ones_bf_d = din("ones_bf", (1, 128), BF16)
    iota8_d = din("iota8", (128, 8))
    nine_m_d = din("nine_m", (128, 8))
    emask2_d = din("emask2", (128, 8))
    epsc_d = din("epsc", (128, 2))

    y_out = nc.dram_tensor("y", [TOK, D], F32, kind="ExternalOutput")
    dbg = {}
    if debug_stage is not None:
        dbg["stream"] = nc.dram_tensor("dbg_stream", [128, NT * D], F32, kind="ExternalOutput")
        dbg["hnorm"] = nc.dram_tensor("dbg_hnorm", [128, 8 * N], BF16, kind="ExternalOutput")
        dbg["ctx"] = nc.dram_tensor("dbg_ctx", [128, 8 * DH], BF16, kind="ExternalOutput")
        dbg["g"] = nc.dram_tensor("dbg_g", [128, 32 * 512], BF16, kind="ExternalOutput")

    es = ExitStack()
    with es:
        p = Prog(nc, es)
        sbT = lambda name, shape, dt=F32: es.enter_context(nc.sbuf_tensor(name, list(shape), dt))

        stream = sbT("stream", (128, NT * D))
        tmp1 = sbT("tmp1", (128, D))[:]
        tbf1 = sbT("tbf1", (128, D), BF16)[:]
        cols = sbT("cols", (128, 32))
        bn12 = sbT("bn12", (128, 12))
        ident_bf = sbT("ident_bf_s", (128, 128), BF16)
        ident_f = sbT("ident_f_s", (128, 128), F32)
        ones_bf = sbT("ones_bf_s", (1, 128), BF16)
        iota8 = sbT("iota8_s", (128, 8))
        nine_m = sbT("nine_m_s", (128, 8))
        emask2 = sbT("emask2_s", (128, 8))
        bff1_fm = sbT("bff1_fm_s", (128, 32))
        epsc = sbT("epsc_s", (128, 2))
        pva = sbT("pva", (128, D))
        es_bc = ExitStack()
        pvb = es_bc.enter_context(nc.sbuf_tensor("pvb", [128, D], F32))
        pvc = es_bc.enter_context(nc.sbuf_tensor("pvc", [128, D], F32))
        es.enter_context(es_bc)

        def dma(eng, dst, src, sem, waits=()):
            return p.op(eng, lambda e, d=dst, s=src: e.dma_start(out=d, in_=s),
                        waits=waits, inc=sem, dma=True)

        def ln_stats(x_ap, eps, waits=()):
            for j in range(2):
                p.op("vector", lambda e, j=j, x=x_ap: e.bn_stats(
                        bn12[:, 6 * j:6 * j + 6], x[:, 512 * j:512 * (j + 1)]),
                     waits=waits, inc="V")
            p.op("vector", lambda e: e.bn_aggr(cols[:, 0:2], bn12[:, 0:12]),
                 waits=[("V", p.c("V"))], inc="V")
            ec = 0 if eps == EPS_LN else 1
            p.op("scalar", lambda e, ec=ec: e.activation(
                    cols[:, 3:4], cols[:, 1:2], AF.Sqrt, bias=epsc[:, ec:ec + 1]),
                 waits=[("V", p.c("V"))], inc="S")
            p.op("vector", lambda e: e.reciprocal(cols[:, 2:3], cols[:, 3:4]),
                 waits=[("S", p.c("S"))], inc="V")

        def rms_stats(x_ap, eps, waits=()):
            for j in range(2):
                p.op("vector", lambda e, j=j, x=x_ap: e.bn_stats(
                        bn12[:, 6 * j:6 * j + 6], x[:, 512 * j:512 * (j + 1)]),
                     waits=waits, inc="V")
            p.op("vector", lambda e: e.bn_aggr(cols[:, 0:2], bn12[:, 0:12]),
                 waits=[("V", p.c("V"))], inc="V")
            p.op("vector", lambda e: e.scalar_tensor_tensor(
                    cols[:, 4:5], cols[:, 0:1], cols[:, 0:1], cols[:, 1:2],
                    op0=ALU.mult, op1=ALU.add),
                 waits=[("V", p.c("V"))], inc="V")
            ec = 0 if eps == EPS_LN else 1
            p.op("scalar", lambda e, ec=ec: e.activation(
                    cols[:, 3:4], cols[:, 4:5], AF.Sqrt, bias=epsc[:, ec:ec + 1]),
                 waits=[("V", p.c("V"))], inc="S")
            p.op("vector", lambda e: e.reciprocal(cols[:, 2:3], cols[:, 3:4]),
                 waits=[("S", p.c("S"))], inc="V")

        def cast_transpose_store(src_f32, dst, dst_off, dst_stride, pt_bank, extra_waits=()):
            p.op("scalar", lambda e, s=src_f32: e.activation(tbf1[:], s, AF.Copy),
                 waits=list(extra_waits) + [("T", p.c("T"))], inc="S")
            sh = p.c("S")
            for j in range(8):
                p.op("tensor", lambda e, j=j, b=pt_bank: e.transpose(
                        b[:, j * 128:(j + 1) * 128], tbf1[:, j * 128:(j + 1) * 128], ident_bf[:]),
                     waits=[("S", sh)] if j == 0 else (), inc="T" if j == 7 else None)
            th = p.c("T")
            p.op("vector", lambda e, b=pt_bank: e.tensor_copy(
                    bass.AP(dst, dst_off, [[dst.shape[1], 128], [dst_stride, 8], [1, 128]]),
                    b[:]),
                 waits=[("T", th)], inc="V")
            p.op("tensor", lambda e: None, waits=[("V", p.c("V"))])

        dma("sync", ident_bf[:], ident_bf_d[:], "DX")
        dma("sync", ident_f[:], ident_f_d[:], "DX")
        dma("sync", ones_bf[:], ones_bf_d[:], "DX")
        dma("sync", iota8[:], iota8_d[:], "DX")
        dma("sync", nine_m[:], nine_m_d[:], "DX")
        dma("sync", emask2[:], emask2_d[:], "DX")
        dma("sync", epsc[:], epsc_d[:], "DX")
        dma("sync", bff1_fm[:], bff1_fm_d[:], "DX")
        dma("sync", pva[:], ln1g_t[:], "DX")
        dma("sync", pvb[:], ln1b_t[:], "DX")
        init_consts = p.c("DX")
        p.op("sync", lambda e: None, waits=[("DX", init_consts)])
        p.flush_block()

        esA0 = ExitStack()
        es.enter_context(esA0)
        h_norm = esA0.enter_context(nc.sbuf_tensor("h_norm", [128, 8 * N], BF16))
        if True:
            ctx = esA0.enter_context(nc.sbuf_tensor("ctx", [128, 8 * DH], BF16))
            if True:
                esA1 = ExitStack()
                esA0.enter_context(esA1)
                _a = esA1.enter_context
                xs = _a(nc.sbuf_tensor("xs", [128, D], F32))
                wkh = _a(nc.sbuf_tensor("wkh", [128, 8 * DH], BF16))
                wvh = _a(nc.sbuf_tensor("wvh", [128, 8 * DH], BF16))
                kraw = _a(nc.sbuf_tensor("kraw", [128, N], F32))
                expbf = _a(nc.sbuf_tensor("expbf", [128, N], BF16))
                ktm = _a(nc.sbuf_tensor("ktm", [128, NTF * DH], BF16))
                vbf = _a(nc.sbuf_tensor("vbf", [128, 2 * DH], BF16))
                rsk = _a(nc.sbuf_tensor("rsk", [128, 8], F32))
                pa = _a(nc.psum_tensor("pa", [128, 512], F32))
                pb = _a(nc.psum_tensor("pb", [128, 512], F32))
                pc = _a(nc.psum_tensor("pc", [128, 512], F32))
                pd = _a(nc.psum_tensor("pd", [128, 512], F32))
                pe2 = _a(nc.psum_tensor("pe2", [128, 512], F32))
                pt1 = _a(nc.psum_tensor("pt1", [128, 1024], BF16))
                if True:

                    pt1b = pt1
                    wk_done0 = dma("gpsimd", wkh[:], wk_h[0], "DW")
                    wv_done0 = dma("gpsimd", wvh[:], wv_h[0], "DW")
                    kc0_evacs = 0
                    hn_done = {}

                    def s0_stats(t):
                        par = t % 2
                        cb = par * 16
                        xbuf = xs[:]
                        wx = dma("sync", xbuf, x_core[t * 128:(t + 1) * 128, :], "DX",
                                 waits=[("V", p.c("V"))])
                        for jj in range(2):
                            p.op("vector", lambda e, jj=jj, x=xbuf: e.bn_stats(
                                    bn12[:, 6 * jj:6 * jj + 6], x[:, 512 * jj:512 * (jj + 1)]),
                                 waits=[("DX", wx)], inc="V")
                        p.op("vector", lambda e, cb=cb: e.bn_aggr(cols[:, cb:cb + 2], bn12[:, 0:12]),
                             waits=[("V", p.c("V"))], inc="V")
                        p.op("scalar", lambda e, cb=cb: e.activation(
                                cols[:, cb + 3:cb + 4], cols[:, cb + 1:cb + 2], AF.Sqrt,
                                bias=epsc[:, 0:1]),
                             waits=[("V", p.c("V"))], inc="S")
                        p.op("vector", lambda e, cb=cb: e.reciprocal(
                                cols[:, cb + 2:cb + 3], cols[:, cb + 3:cb + 4]),
                             waits=[("S", p.c("S"))], inc="V")
                        if t < NT:
                            u_dst = stream[:, t * D:(t + 1) * D]
                        else:
                            u_dst = expbf[:, par * 1024:(par + 1) * 1024]
                        p.op("vector", lambda e, xb=xbuf, ud=u_dst, cb=cb: e.tensor_scalar(
                                ud, xb, cols[:, cb:cb + 1], cols[:, cb + 2:cb + 3],
                                op0=ALU.subtract, op1=ALU.mult),
                             waits=[("V", p.c("V")), ("S", p.c("S")), ("T", p.c("T"))], inc="V")
                        p.op("vector", lambda e, cb=cb: e.scalar_tensor_tensor(
                                cols[:, cb + 4:cb + 5], cols[:, cb + 2:cb + 3],
                                cols[:, cb + 2:cb + 3], cols[:, cb + 1:cb + 2],
                                op0=ALU.mult, op1=ALU.mult),
                             waits=[("V", p.c("V"))], inc="V")
                        p.op("scalar", lambda e, cb=cb: e.activation(
                                cols[:, cb + 5:cb + 6], cols[:, cb + 4:cb + 5], AF.Sqrt,
                                bias=epsc[:, 0:1]),
                             waits=[("V", p.c("V"))], inc="S")
                        p.op("vector", lambda e, cb=cb: e.reciprocal(
                                cols[:, cb + 6:cb + 7], cols[:, cb + 5:cb + 6]),
                             waits=[("S", p.c("S"))], inc="V")
                        return u_dst

                    def s0_apply(t, u_src):
                        par = t % 2
                        cb = par * 16
                        p.op("vector", lambda e, us=u_src, cb=cb: e.tensor_scalar(
                                tmp1, us, cols[:, cb + 6:cb + 7], None, op0=ALU.mult),
                             waits=[("V", p.c("V")), ("S", p.c("S"))], inc="V")
                        p.op("vector", lambda e: e.tensor_mul(tmp1, tmp1, pva[:]),
                             waits=[("V", p.c("V"))], inc="V")
                        p.op("vector", lambda e: e.tensor_add(tmp1, tmp1, pvb[:]),
                             waits=[("V", p.c("V"))], inc="V")
                        cast_transpose_store(tmp1, h_norm, t * 128, N, pt1,
                                             extra_waits=[("V", p.c("V"))])
                        hn_done[t] = p.c("V")
                        if t % 4 == 3:
                            nb = t // 4
                            bank = pa if nb % 2 == 0 else pb
                            for kc in range(8):
                                st = wkh[:, kc * DH: kc * DH + 128]
                                mv = h_norm[:, kc * N + nb * 512: kc * N + (nb + 1) * 512]
                                p.op("tensor", lambda e, b=bank, s=st, m=mv, k=kc: e.matmul(
                                        b[:], s, m, start=(k == 0), stop=(k == 7)),
                                     waits=([("DW", wk_done0), ("V", hn_done[t])] if kc == 0 else ()),
                                     inc="T" if kc == 7 else None)
                            tmm = p.c("T")
                            p.op("vector", lambda e, b=bank, nb=nb: e.tensor_copy(
                                    kraw[:, nb * 512:(nb + 1) * 512], b[:]),
                                 waits=[("T", tmm), ("S", p.c("S"))], inc="V")
                            nonlocal_kc0[0] = p.c("V")

                    nonlocal_kc0 = [0]
                    upend = []
                    for t in range(NTF):
                        upend.append((t, s0_stats(t)))
                        if len(upend) > 1:
                            s0_apply(*upend.pop(0))
                    while upend:
                        s0_apply(*upend.pop(0))
                    kc0_evacs = nonlocal_kc0[0]
                    if debug_stage == "s0":
                        p.op("sync", lambda e: e.dma_start(out=dbg["stream"][:], in_=stream[:]),
                             waits=[("V", p.c("V"))], inc="DO", dma=True)
                        p.op("sync", lambda e: e.dma_start(out=dbg["hnorm"][:], in_=h_norm[:]),
                             waits=[("V", p.c("V"))], inc="DO", dma=True)
                        p.op("sync", lambda e: None, waits=[("DO", p.c("DO"))])
                        p.flush_block()
                        return nc

                    for h in range(HEADS):
                        if h > 0:
                            wk_done = dma("gpsimd", wkh[:], wk_h[h], "DW",
                                          waits=[("T", p.c("T"))])
                            wv_done = dma("gpsimd", wvh[:], wv_h[h], "DW")
                        else:
                            wk_done, wv_done = wk_done0, wv_done0
                        if h == 1:
                            for (dstp, srcp) in ((wff1bf, wff1_img), (actbf, actstream),
                                                 (w2bf, w2t_img), (wff2bf, wff2_img)):
                                npc = dstp.shape[1]
                                for q in range(4):
                                    dma("gpsimd", dstp[:, q * npc // 4:(q + 1) * npc // 4],
                                        srcp[:, q * npc // 4:(q + 1) * npc // 4], "DC")
                        for c in range(2):
                            if h == 0 and c == 0:
                                vk = kc0_evacs
                                skip_mm = True
                            else:
                                skip_mm = False
                            for nb in (() if skip_mm else range(8)):
                                bank = pa if nb % 2 == 0 else pb
                                for kc in range(8):
                                    st = wkh[:, kc * DH + c * 128: kc * DH + (c + 1) * 128]
                                    mv = h_norm[:, kc * N + nb * 512: kc * N + (nb + 1) * 512]
                                    p.op("tensor", lambda e, b=bank, s=st, m=mv, k=kc: e.matmul(
                                            b[:], s, m, start=(k == 0), stop=(k == 7)),
                                         waits=([("DW", wk_done), ("V", p.c("V"))] if kc == 0 else ()),
                                         inc="T" if kc == 7 else None)
                                tmm = p.c("T")
                                p.op("vector", lambda e, b=bank, nb=nb: e.tensor_copy(
                                        kraw[:, nb * 512:(nb + 1) * 512], b[:]),
                                     waits=[("T", tmm), ("S", p.c("S"))], inc="V")
                            vk = p.c("V")
                            p.op("vector", lambda e: e.reduce_max(
                                    cols[:, 8:9], kraw[:], axis=AX.X, negate=True),
                                 waits=[("V", vk)], inc="V")
                            p.op("vector", lambda e: e.tensor_scalar(
                                    cols[:, 9:10], cols[:, 8:9], SCALE_QK, None, op0=ALU.mult),
                                 waits=[("V", p.c("V"))], inc="V")
                            vb2 = p.c("V")
                            p.op("scalar", lambda e: e.activation(
                                    expbf[:], kraw[:], AF.Exp, bias=cols[:, 9:10], scale=SCALE_QK,
                                    accum_out=cols[:, 10:11]),
                                 waits=[("V", vb2), ("T", p.c("T"))], inc="S")
                            p.op("vector", lambda e, c=c, h=h: e.reciprocal(
                                    rsk[:, h * 2 + c: h * 2 + c + 1], cols[:, 10:11]),
                                 waits=[("S", p.c("S"))], inc="V")
                            for tg in range(4):
                                for j in range(8):
                                    t = tg * 8 + j
                                    p.op("tensor", lambda e, t=t, j=j: e.transpose(
                                            pt1b[:, j * 128:(j + 1) * 128],
                                            expbf[:, t * 128:(t + 1) * 128], ident_bf[:]),
                                         waits=[("S", p.c("S"))] if (tg == 0 and j == 0) else (),
                                         inc="T" if j == 7 else None)
                                tt = p.c("T")
                                p.op("vector", lambda e, tg=tg, c=c: e.tensor_copy(
                                        bass.AP(ktm, tg * 8 * DH + c * 128,
                                                [[NTF * DH, 128], [DH, 8], [1, 128]]),
                                        pt1b[:]),
                                     waits=[("T", tt)], inc="V")
                                p.op("tensor", lambda e: None, waits=[("V", p.c("V"))])
                        kt_ready = p.c("V")
                        vpend = []

                        def emit_ctx(t2m, vb2, vv2):
                            for c in range(2):
                                bank = pd if c == 0 else pe2
                                p.op("tensor", lambda e, b=bank, t=t2m, c=c, vb=vb2: e.matmul(
                                        b[:, 0:DH],
                                        ktm[:, t * DH + c * 128: t * DH + (c + 1) * 128], vb,
                                        start=(t == 0), stop=(t == NTF - 1)),
                                     waits=[("V", vv2)], inc="T" if c == 1 else None)

                        for t in range(NTF):
                            vb = vbf[:, (t % 2) * DH:(t % 2 + 1) * DH]
                            for kc in range(8):
                                st = h_norm[:, kc * N + t * 128: kc * N + (t + 1) * 128]
                                mv = wvh[:, kc * DH:(kc + 1) * DH]
                                p.op("tensor", lambda e, s=st, m=mv, k=kc: e.matmul(
                                        pc[:, 0:DH], s, m, start=(k == 0), stop=(k == 7)),
                                     waits=([("DW", wv_done), ("V", p.c("V"))] if kc == 0 else ()),
                                     inc="T" if kc == 7 else None)
                            tv = p.c("T")
                            p.op("vector", lambda e, vb=vb: e.tensor_copy(vb, pc[:, 0:DH]),
                                 waits=[("T", tv), ("V", kt_ready)], inc="V")
                            vpend.append((t, vb, p.c("V")))
                            if len(vpend) > 1:
                                emit_ctx(*vpend.pop(0))
                        while vpend:
                            emit_ctx(*vpend.pop(0))
                        tctx = p.c("T")
                        for c in range(2):
                            bank = pd if c == 0 else pe2
                            p.op("vector", lambda e, b=bank, c=c, h=h: e.tensor_scalar(
                                    ctx[:, (h * 2 + c) * DH:(h * 2 + c + 1) * DH], b[:, 0:DH],
                                    rsk[:, h * 2 + c: h * 2 + c + 1], None, op0=ALU.mult),
                                 waits=[("T", tctx)], inc="V")
                        p.op("tensor", lambda e: None, waits=[("V", p.c("V"))])
                    if debug_stage == "a1":
                        p.op("sync", lambda e: e.dma_start(out=dbg["ctx"][:], in_=ctx[:]),
                             waits=[("V", p.c("V"))], inc="DO", dma=True)
                        p.op("sync", lambda e: None, waits=[("DO", p.c("DO"))])
                        p.flush_block()
                        return nc
                    p.flush_block()

                esA1.close()
                with (
                    nc.sbuf_tensor("wqh", [128, 8 * DH], BF16) as wqh,
                    nc.sbuf_tensor("woh", [128, 2 * 1024], BF16) as woh,
                    nc.sbuf_tensor("qfm", [128, 2 * TOK], BF16) as qfm,
                    nc.sbuf_tensor("attnb", [128, 2 * TOK], BF16) as attnb,
                    nc.sbuf_tensor("eqbf", [128, 2 * DH], F32) as eqbf,
                    nc.psum_tensor("pa_2", [128, 512], F32) as pa,
                    nc.psum_tensor("pb_2", [128, 512], F32) as pb,
                    nc.psum_tensor("pc_2", [128, 512], F32) as pc,
                    nc.psum_tensor("pd_2", [128, 512], F32) as pd,
                    nc.psum_tensor("pt1b_2", [128, 1024], BF16) as pt1b,
                ):
                    bo_done = dma("sync", pvc[:], bo_t[:], "DX")
                    for h in range(HEADS):
                        wq_done = dma("gpsimd", wqh[:], wq_h[h], "DW",
                                      waits=[("T", p.c("T"))])
                        wo_done = dma("gpsimd", woh[:], wo_img[:, 2 * h * 1024:(2 * h + 2) * 1024], "DW")
                        q_evac = [0, 0]
                        q_soft = [0, 0]
                        pend = []

                        def emit_qtrans(t, par, vq):
                            for j in range(2):
                                p.op("tensor", lambda e, j=j, par=par: e.transpose(
                                        pt1b[:, par * 256 + j * 128: par * 256 + (j + 1) * 128],
                                        tbf1[:, par * DH + j * 128: par * DH + (j + 1) * 128],
                                        ident_bf[:]),
                                     waits=[("V", vq)], inc="T" if j == 1 else None)
                            tq2 = p.c("T")
                            p.op("vector", lambda e, t=t, par=par: e.tensor_copy(
                                    bass.AP(qfm, t * 128, [[2 * TOK, 128], [TOK, 2], [1, 128]]),
                                    pt1b[:, par * 256:par * 256 + 256]),
                                 waits=[("T", tq2)], inc="V")
                            q_evac[par] = p.c("V")
                        for t in range(NT):
                            par = t % 2
                            qb = pa if par == 0 else pd
                            cb = 8 if par == 0 else 20
                            for kc in range(8):
                                st = h_norm[:, kc * N + t * 128: kc * N + (t + 1) * 128]
                                mv = wqh[:, kc * DH:(kc + 1) * DH]
                                p.op("tensor", lambda e, qb=qb, s=st, m=mv, k=kc: e.matmul(
                                        qb[:, 0:DH], s, m, start=(k == 0), stop=(k == 7)),
                                     waits=([("DW", wq_done), ("S", q_soft[par]),
                                             ("V", q_evac[par])] if kc == 0 else ()),
                                     inc="T" if kc == 7 else None)
                            tq = p.c("T")
                            p.op("vector", lambda e, qb=qb, cb=cb: e.reduce_max(
                                    cols[:, cb:cb + 1], qb[:, 0:DH], axis=AX.X, negate=True),
                                 waits=[("T", tq)], inc="V")
                            p.op("vector", lambda e, cb=cb: e.tensor_scalar(
                                    cols[:, cb + 1:cb + 2], cols[:, cb:cb + 1], SCALE_QK,
                                    None, op0=ALU.mult),
                                 waits=[("V", p.c("V"))], inc="V")
                            p.op("scalar", lambda e, qb=qb, cb=cb, par=par: e.activation(
                                    eqbf[:, par * DH:(par + 1) * DH], qb[:, 0:DH], AF.Exp,
                                    bias=cols[:, cb + 1:cb + 2],
                                    scale=SCALE_QK, accum_out=cols[:, cb + 2:cb + 3]),
                                 waits=[("V", p.c("V"))], inc="S")
                            p.op("vector", lambda e, cb=cb: e.reciprocal(
                                    cols[:, cb + 3:cb + 4], cols[:, cb + 2:cb + 3]),
                                 waits=[("S", p.c("S"))], inc="V")
                            p.op("vector", lambda e, cb=cb, par=par: e.tensor_scalar(
                                    tbf1[:, par * DH:(par + 1) * DH],
                                    eqbf[:, par * DH:(par + 1) * DH],
                                    cols[:, cb + 3:cb + 4], None, op0=ALU.mult),
                                 waits=[("V", p.c("V")), ("T", p.c("T"))], inc="V")
                            q_soft[par] = p.c("S")
                            pend.append((t, par, p.c("V")))
                            if len(pend) > 1:
                                emit_qtrans(*pend.pop(0))
                        while pend:
                            emit_qtrans(*pend.pop(0))
                        vqall = p.c("V")
                        for ec in range(2):
                            for nb in range(4):
                                for c in range(2):
                                    st = ctx[:, (h * 2 + c) * DH + ec * 128:
                                             (h * 2 + c) * DH + (ec + 1) * 128]
                                    mv = qfm[:, c * TOK + nb * 512: c * TOK + (nb + 1) * 512]
                                    p.op("tensor", lambda e, s=st, m=mv, c=c: e.matmul(
                                            pb[:], s, m, start=(c == 0), stop=(c == 1)),
                                         waits=([("V", vqall if (ec == 0 and nb == 0) else p.c("V"))]
                                                if c == 0 else ()),
                                         inc="T" if c == 1 else None)
                                ta = p.c("T")
                                p.op("vector", lambda e, ec=ec, nb=nb: e.tensor_copy(
                                        attnb[:, ec * TOK + nb * 512: ec * TOK + (nb + 1) * 512],
                                        pb[:]),
                                     waits=[("T", ta)], inc="V")
                        vat = p.c("V")
                        for t in range(NT):
                            for half in range(2):
                                bank = pc if half == 0 else pd
                                for ec in range(2):
                                    st = attnb[:, ec * TOK + t * 128: ec * TOK + (t + 1) * 128]
                                    mv = woh[:, ec * 1024 + half * 512: ec * 1024 + (half + 1) * 512]
                                    p.op("tensor", lambda e, b=bank, s=st, m=mv, ec=ec: e.matmul(
                                            b[:], s, m, start=(ec == 0), stop=(ec == 1)),
                                         waits=(([("DW", wo_done), ("V", vat)]
                                                 if (t == 0 and half == 0) else [("V", p.c("V"))])
                                                if ec == 0 else ()),
                                         inc="T" if ec == 1 else None)
                                tw = p.c("T")
                                dst = stream[:, t * D + half * 512: t * D + (half + 1) * 512]
                                p.op("vector", lambda e, b=bank, d=dst: e.scalar_tensor_tensor(
                                        d, b[:], 1.0, d, op0=ALU.mult, op1=ALU.add),
                                     waits=[("T", tw)], inc="V")
                        p.op("tensor", lambda e: None, waits=[("V", p.c("V"))])
                    for t in range(NT):
                        p.op("vector", lambda e, t=t: e.tensor_add(
                                stream[:, t * D:(t + 1) * D], stream[:, t * D:(t + 1) * D], pvc[:]),
                             waits=([("DX", bo_done), ("V", p.c("V"))] if t == 0 else ()), inc="V")
                    if debug_stage == "a2":
                        p.op("sync", lambda e: e.dma_start(out=dbg["stream"][:], in_=stream[:]),
                             waits=[("V", p.c("V"))], inc="DO", dma=True)
                        p.op("sync", lambda e: None, waits=[("DO", p.c("DO"))])
                        p.flush_block()
                        return nc
                    p.flush_block()

        esA0.close()
        with (
            nc.sbuf_tensor("h2fm", [128, 8 * TOK], BF16) as h2fm,
            nc.sbuf_tensor("gfm", [128, 32 * TB], BF16) as gfm,
            nc.sbuf_tensor("wff1b", [128, 2 * 8 * 512], BF16) as wff1b,
            nc.sbuf_tensor("wff2b", [128, 2 * 4 * 512], BF16) as wff2b,
            nc.psum_tensor("pa2", [128, 512], F32) as pa2,
            nc.psum_tensor("pb2", [128, 512], F32) as pb2,
            nc.psum_tensor("pf0", [128, 512], F32) as pf0,
            nc.psum_tensor("pf1", [128, 512], F32) as pf1,
            nc.psum_tensor("pf2", [128, 512], F32) as pf2,
            nc.psum_tensor("pf3", [128, 512], F32) as pf3,
            nc.psum_tensor("pt2", [128, 1024], BF16) as pt2,
        ):
            pfb = (pf0, pf1, pf2, pf3)
            g1_done = dma("sync", pva[:], ln2g_t[:], "DX")
            g2_done = dma("sync", pvb[:], ln2b_t[:], "DX")
            bf2_done = dma("sync", pvc[:], bff2_t[:], "DX")
            h2v = {}

            def b_ln2(t):
                xt = stream[:, t * D:(t + 1) * D]
                ln_stats(xt, EPS_LN, [("DX", g2_done)])
                p.op("vector", lambda e, xt=xt: e.tensor_scalar(
                        tmp1, xt, cols[:, 0:1], cols[:, 2:3], op0=ALU.subtract, op1=ALU.mult),
                     waits=[("V", p.c("V")), ("S", p.c("S"))], inc="V")
                p.op("vector", lambda e: e.tensor_mul(tmp1, tmp1, pva[:]),
                     waits=[("V", p.c("V"))], inc="V")
                p.op("vector", lambda e: e.tensor_add(tmp1, tmp1, pvb[:]),
                     waits=[("V", p.c("V"))], inc="V")
                cast_transpose_store(tmp1, h2fm, t * 128, TOK, pt2,
                                     extra_waits=[("V", p.c("V"))])
                h2v[t] = p.c("V")

            for t in range(4):
                b_ln2(t)
            g1hist = []
            g2hist = []
            for tb in range(NTB):
                for t in range(4 * tb + 4, 4 * tb + 8):
                    if t < NT:
                        b_ln2(t)
                for gg in range(8):
                    buf1 = wff1b[:, (gg % 2) * 8 * 512:(gg % 2 + 1) * 8 * 512]
                    ld1 = dma("sync", buf1,
                              bass.AP(wff1bf, gg * 512, [[8 * FF, 128], [FF, 8], [1, 512]]),
                              "DS", waits=([("T", g1hist[-2])] if len(g1hist) >= 2 else []) +
                                    [("DC", p.c("DC"))])
                    for gj in range(4):
                        g = gg * 4 + gj
                        bank = pa2 if gj % 2 == 0 else pb2
                        for kc in range(8):
                            st = buf1[:, kc * 512 + gj * 128: kc * 512 + (gj + 1) * 128]
                            mv = h2fm[:, kc * TOK + tb * TB: kc * TOK + (tb + 1) * TB]
                            p.op("tensor", lambda e, b=bank, s=st, m=mv, k=kc: e.matmul(
                                    b[:], s, m, start=(k == 0), stop=(k == 7)),
                                 waits=([("DS", ld1), ("S", p.c("S")),
                                         ("V", h2v[4 * tb + 3])] if kc == 0 else ()),
                                 inc="T" if kc == 7 else None)
                        tg = p.c("T")
                        p.op("scalar", lambda e, b=bank, g=g: e.activation(
                                gfm[:, g * 512:(g + 1) * 512], b[:], AF.Gelu_apprx_tanh,
                                bias=bff1_fm[:, g:g + 1]),
                             waits=[("T", tg)], inc="S")
                    g1hist.append(p.c("T"))
                sgelu = p.c("S")
                if debug_stage == "b1" and tb == 0:
                    p.op("sync", lambda e: e.dma_start(out=dbg["hnorm"][:, 0:8 * TOK], in_=h2fm[:]),
                         waits=[("S", sgelu)], inc="DO", dma=True)
                    p.op("sync", lambda e: e.dma_start(out=dbg["g"][:], in_=gfm[:]),
                         waits=[("S", sgelu)], inc="DO", dma=True)
                    p.op("sync", lambda e: None, waits=[("DO", p.c("DO"))])
                    p.flush_block()
                    return nc
                for half in range(2):
                    for gc in range(32):
                        if gc % 4 == 0:
                            prw = (gc // 4) % 2
                            buf2 = wff2b[:, prw * 4 * 512:(prw + 1) * 4 * 512]
                            ld2 = dma("scalar", buf2,
                                      bass.AP(wff2bf, gc * 1024 + half * 512,
                                              [[32 * 1024, 128], [1024, 4], [1, 512]]),
                                      "DT",
                                      waits=([("T", g2hist[-2])] if len(g2hist) >= 2 else []) +
                                            [("DC", p.c("DC"))])
                        prw = (gc // 4) % 2
                        for tt in range(4):
                            st = gfm[:, gc * 512 + tt * 128: gc * 512 + (tt + 1) * 128]
                            mv = wff2b[:, prw * 4 * 512 + (gc % 4) * 512:
                                       prw * 4 * 512 + (gc % 4 + 1) * 512]
                            p.op("tensor", lambda e, b=pfb[tt], s=st, m=mv, g=gc: e.matmul(
                                    b[:], s, m, start=(g == 0), stop=(g == 31)),
                                 waits=([("DT", ld2), ("S", sgelu), ("V", p.c("V"))]
                                        if (gc % 4 == 0 and tt == 0) else ()),
                                 inc="T" if (gc == 31 or (gc % 4 == 3 and tt == 3)) else None)
                        if gc % 4 == 3:
                            g2hist.append(p.c("T"))
                    tf = p.c("T")
                    for tt in range(4):
                        t = tb * 4 + tt
                        dst = stream[:, t * D + half * 512: t * D + (half + 1) * 512]
                        p.op("vector", lambda e, b=pfb[tt], d=dst: e.scalar_tensor_tensor(
                                d, b[:], 1.0, d, op0=ALU.mult, op1=ALU.add),
                             waits=[("T", tf), ("DX", bf2_done)], inc="V")
                        p.op("vector", lambda e, d=dst, h2=half: e.tensor_add(
                                d, d, pvc[:, h2 * 512:(h2 + 1) * 512]),
                             waits=[("V", p.c("V"))], inc="V")
                    p.op("tensor", lambda e: None, waits=[("V", p.c("V"))])
            if debug_stage == "b":
                p.op("sync", lambda e: e.dma_start(out=dbg["stream"][:], in_=stream[:]),
                     waits=[("V", p.c("V"))], inc="DO", dma=True)
                p.op("sync", lambda e: None, waits=[("DO", p.c("DO"))])
                p.flush_block()
                return nc
            p.flush_block()

        with (
            nc.sbuf_tensor("wdb", [128, 8 * 1024], BF16) as wdb,
            nc.sbuf_tensor("x2fm", [128, 2 * 1024], BF16) as x2fm,
            nc.sbuf_tensor("xo", [128, 2 * D], F32) as xo,
            nc.psum_tensor("pa3", [128, 512], F32) as pa3,
            nc.psum_tensor("pb3", [128, 512], F32) as pb3,
            nc.psum_tensor("pt3", [128, 1024], BF16) as pt3,
        ):
            wd_done = dma("gpsimd", wdb[:], wd_img[:], "DW")
            bd_done = dma("sync", pva[:], bd_t[:], "DX")
            x2v = {}
            xo_done = {}

            def c_front(t):
                xt = stream[:, t * D:(t + 1) * D]
                xob = xo[:, (t % 2) * D:(t % 2 + 1) * D]
                xo_done[t] = dma("sync", xob, x_core[t * 128:(t + 1) * 128, :], "DX",
                                 waits=[("V", p.c("V"))])
                p.op("scalar", lambda e, xt=xt: e.activation(tbf1[:], xt, AF.Copy),
                     waits=[("V", p.c("V")), ("T", p.c("T"))], inc="S")
                sh = p.c("S")
                for j2 in range(8):
                    p.op("tensor", lambda e, j2=j2, t=t: e.transpose(
                            pt3[:, j2 * 128:(j2 + 1) * 128], tbf1[:, j2 * 128:(j2 + 1) * 128],
                            ident_bf[:]),
                         waits=[("S", sh)] if j2 == 0 else (), inc="T" if j2 == 7 else None)
                p.op("vector", lambda e, t=t: e.tensor_copy(
                        x2fm[:, (t % 2) * 1024:(t % 2 + 1) * 1024], pt3[:]),
                     waits=[("T", p.c("T"))], inc="V")
                x2v[t] = p.c("V")

            def c_back(t):
                xt = stream[:, t * D:(t + 1) * D]
                xob = xo[:, (t % 2) * D:(t % 2 + 1) * D]
                for half in range(2):
                    bank = pa3 if half == 0 else pb3
                    for kc in range(8):
                        st = x2fm[:, (t % 2) * 1024 + kc * 128:(t % 2) * 1024 + (kc + 1) * 128]
                        mv = wdb[:, kc * 1024 + half * 512: kc * 1024 + (half + 1) * 512]
                        p.op("tensor", lambda e, b=bank, s=st, m=mv, k=kc: e.matmul(
                                b[:], s, m, start=(k == 0), stop=(k == 7)),
                             waits=([("V", x2v[t]), ("DW", wd_done), ("V", p.c("V"))]
                                    if kc == 0 else ()),
                             inc="T" if kc == 7 else None)
                    tw = p.c("T")
                    p.op("vector", lambda e, b=bank, h2=half: e.tensor_add(
                            tmp1[:, h2 * 512:(h2 + 1) * 512], b[:],
                            pva[:, h2 * 512:(h2 + 1) * 512]),
                         waits=[("T", tw), ("DX", bd_done)], inc="V")
                ln_stats(tmp1, EPS_LN, [("V", p.c("V"))])
                p.op("vector", lambda e: e.tensor_scalar(
                        tmp1, tmp1, cols[:, 0:1], cols[:, 2:3], op0=ALU.subtract, op1=ALU.mult),
                     waits=[("V", p.c("V"))], inc="V")
                p.op("vector", lambda e, xt=xt, xob=xob: e.tensor_add(xt, tmp1, xob),
                     waits=[("V", p.c("V")), ("DX", xo_done[t])], inc="V")

            c_front(0)
            for t in range(NT):
                if t + 1 < NT:
                    c_front(t + 1)
                c_back(t)
            if debug_stage == "c":
                p.op("sync", lambda e: e.dma_start(out=dbg["stream"][:], in_=stream[:]),
                     waits=[("V", p.c("V"))], inc="DO", dma=True)
                p.op("sync", lambda e: None, waits=[("DO", p.c("DO"))])
                p.flush_block()
                return nc
            p.flush_block()

        es_bc.close()
        if True:
            _e = es.enter_context
            tfm = _e(nc.sbuf_tensor("tfm", [128, 8 * TB], BF16))
            gateb = _e(nc.sbuf_tensor("gateb", [128, 64], BF16))
            a1tb = _e(nc.sbuf_tensor("a1tb", [128, 8 * 128], BF16))
            a3tb = _e(nc.sbuf_tensor("a3tb", [128, 8 * 128], BF16))
            b2tab = _e(nc.sbuf_tensor("b2tab", [128, 1024], BF16))
            b2tbb = _e(nc.sbuf_tensor("b2tbb", [128, 1024], BF16))
            actb = _e(nc.sbuf_tensor("actb", [128, 2 * 5120], BF16))
            w2b = _e(nc.sbuf_tensor("w2b", [128, 2 * 4096], BF16))
            r2accA = _e(nc.sbuf_tensor("r2accA", [128, TB], F32))
            r2accB = _e(nc.sbuf_tensor("r2accB", [128, TB], F32))
            r1m = _e(nc.sbuf_tensor("r1mS", [128, 8 * TB], BF16))
            r3m = _e(nc.sbuf_tensor("r3mS", [128, 8 * TB], BF16))
            bsb1 = _e(nc.sbuf_tensor("bsb1", [128, 2 * TB], BF16))
            bsb3 = _e(nc.sbuf_tensor("bsb3", [128, 2 * TB], BF16))
            sbuf_s = _e(nc.sbuf_tensor("sbuf_s", [128, 2 * TB], BF16))
            mbuf = _e(nc.sbuf_tensor("mbuf", [128, 2 * TB], BF16))
            t2b = _e(nc.sbuf_tensor("t2b", [128, 2 * TB], BF16))
            wbc = _e(nc.sbuf_tensor("wbc", [128, 8 * TB], BF16))
            wtmp = _e(nc.sbuf_tensor("wtmp", [1, TB], BF16))
            wrall = _e(nc.sbuf_tensor("wrall", [128, 32], F32))
            r2wA = _e(nc.sbuf_tensor("r2wA", [128, TB], BF16))
            r2wB = _e(nc.sbuf_tensor("r2wB", [128, TB], BF16))
            actw = _e(nc.sbuf_tensor("actw", [128, 32 * 512], BF16))
            probs = _e(nc.sbuf_tensor("probs", [128, 8], F32))
            s8 = _e(nc.sbuf_tensor("s8", [128, 8], F32))
            i8f = _e(nc.sbuf_tensor("i8f", [128, 8], F32))
            wrow = _e(nc.sbuf_tensor("wrow", [128, 8], F32))
            pB1 = _e(nc.psum_tensor("pB1", [128, 512], F32))
            pH1a = _e(nc.psum_tensor("pH1a", [128, 512], F32))
            pH1b = _e(nc.psum_tensor("pH1b", [128, 512], F32))
            pH3a = _e(nc.psum_tensor("pH3a", [128, 512], F32))
            pH3b = _e(nc.psum_tensor("pH3b", [128, 512], F32))
            pAW = _e(nc.psum_tensor("pAW", [128, 512], F32))
            pR2A = _e(nc.psum_tensor("pR2A", [128, 512], F32))
            pR2B = _e(nc.psum_tensor("pR2B", [128, 512], F32))
        if True:
            nw_done = dma("sync", pva[:], nw_t[:], "DX")
            gate_done = dma("gpsimd", gateb[:], gate_img[:], "DW")
            a1_done = dma("gpsimd", a1tb[:], a1t_img[:], "DW")
            a3_done = dma("gpsimd", a3tb[:], a3t_img[:], "DW")
            b2a_done = dma("gpsimd", b2tab[:], b2ta_img[:], "DW")
            b2b_done = dma("gpsimd", b2tbb[:], b2tb_img[:], "DW")
            dhist = []
            for tb in range(NTB):
                for tt in range(4):
                    t = tb * 4 + tt
                    yt = stream[:, t * D:(t + 1) * D]
                    ln_stats(yt, EPS_LN, [("DX", nw_done)])
                    p.op("vector", lambda e, yt=yt: e.tensor_scalar(
                            tmp1, yt, cols[:, 0:1], cols[:, 2:3],
                            op0=ALU.subtract, op1=ALU.mult),
                         waits=[("V", p.c("V")), ("S", p.c("S"))], inc="V")
                    rms_stats(tmp1, EPS_RMS, [("V", p.c("V"))])
                    p.op("vector", lambda e: e.tensor_scalar(
                            tmp1, tmp1, cols[:, 2:3], None, op0=ALU.mult),
                         waits=[("V", p.c("V")), ("S", p.c("S"))], inc="V")
                    p.op("vector", lambda e: e.tensor_mul(tmp1, tmp1, pva[:]),
                         waits=[("V", p.c("V")), ("T", p.c("T"))], inc="V")
                    vt = p.c("V")
                    for j in range(8):
                        bank = pH1a if j < 4 else pH1b
                        p.op("tensor", lambda e, j=j, bank=bank: e.transpose(
                                bank[:, (j % 4) * 128:(j % 4 + 1) * 128],
                                tmp1[:, j * 128:(j + 1) * 128], ident_f[:]),
                             waits=[("V", vt)] if j == 0 else (),
                             inc="T" if j % 4 == 3 else None)
                    tpose = p.c("T")
                    for half2 in range(2):
                        bank = pH1a if half2 == 0 else pH1b
                        p.op("vector", lambda e, b=bank, h3=half2, tt=tt: e.tensor_copy(
                                bass.AP(tfm, (h3 * 4) * TB + tt * 128,
                                        [[8 * TB, 128], [TB, 4], [1, 128]]), b[:]),
                             waits=[("T", tpose)], inc="V")
                vtfm = p.c("V")
                for tt in range(4):
                    for kc in range(8):
                        st = tfm[:, kc * TB + tt * 128: kc * TB + (tt + 1) * 128]
                        mv = gateb[:, kc * 8:(kc + 1) * 8]
                        p.op("tensor", lambda e, s=st, m=mv, k=kc: e.matmul(
                                pB1[:, 0:8], s, m, start=(k == 0), stop=(k == 7)),
                             waits=([("DW", gate_done), ("V", p.c("V")), ("S", p.c("S"))]
                                    if kc == 0 else ()),
                             inc="T" if kc == 7 else None)
                    tg = p.c("T")
                    p.op("vector", lambda e: e.reduce_max(
                            cols[:, 8:9], pB1[:, 0:8], axis=AX.X, negate=True),
                         waits=[("T", tg)], inc="V")
                    p.op("scalar", lambda e: e.activation(
                            probs[:], pB1[:, 0:8], AF.Exp, bias=cols[:, 8:9],
                            accum_out=cols[:, 10:11]),
                         waits=[("V", p.c("V"))], inc="S")
                    p.op("vector", lambda e: e.reciprocal(cols[:, 11:12], cols[:, 10:11]),
                         waits=[("S", p.c("S"))], inc="V")
                    p.op("vector", lambda e: e.tensor_scalar(
                            probs[:], probs[:], cols[:, 11:12], None, op0=ALU.mult),
                         waits=[("V", p.c("V"))], inc="V")
                    p.op("vector", lambda e: e.reduce_max(cols[:, 12:13], probs[:], axis=AX.X),
                         waits=[("V", p.c("V"))], inc="V")
                    p.op("vector", lambda e: e.tensor_scalar(
                            s8[:], probs[:], cols[:, 12:13], None, op0=ALU.is_equal),
                         waits=[("V", p.c("V"))], inc="V")
                    p.op("vector", lambda e: e.tensor_mul(s8[:], s8[:], nine_m[:]),
                         waits=[("V", p.c("V"))], inc="V")
                    p.op("vector", lambda e: e.tensor_scalar(
                            s8[:], s8[:], -1.0, 9.0, op0=ALU.mult, op1=ALU.add),
                         waits=[("V", p.c("V"))], inc="V")
                    p.op("vector", lambda e: e.tensor_reduce(
                            cols[:, 16:17], s8[:], axis=AX.X, op=ALU.min),
                         waits=[("V", p.c("V"))], inc="V")
                    p.op("vector", lambda e: e.tensor_scalar(
                            s8[:], iota8[:], cols[:, 16:17], None, op0=ALU.is_equal),
                         waits=[("V", p.c("V"))], inc="V")
                    p.op("vector", lambda e: e.tensor_mul(i8f[:], probs[:], s8[:]),
                         waits=[("V", p.c("V"))], inc="V")
                    p.op("vector", lambda e: e.tensor_sub(i8f[:], probs[:], i8f[:]),
                         waits=[("V", p.c("V"))], inc="V")
                    p.op("vector", lambda e: e.reduce_max(cols[:, 13:14], i8f[:], axis=AX.X),
                         waits=[("V", p.c("V"))], inc="V")
                    p.op("vector", lambda e: e.tensor_scalar(
                            i8f[:], i8f[:], cols[:, 13:14], None, op0=ALU.is_equal),
                         waits=[("V", p.c("V"))], inc="V")
                    p.op("vector", lambda e: e.tensor_mul(i8f[:], i8f[:], nine_m[:]),
                         waits=[("V", p.c("V"))], inc="V")
                    p.op("vector", lambda e: e.tensor_scalar(
                            i8f[:], i8f[:], -1.0, 9.0, op0=ALU.mult, op1=ALU.add),
                         waits=[("V", p.c("V"))], inc="V")
                    p.op("vector", lambda e: e.tensor_reduce(
                            cols[:, 17:18], i8f[:], axis=AX.X, op=ALU.min),
                         waits=[("V", p.c("V"))], inc="V")
                    p.op("vector", lambda e: e.tensor_scalar(
                            i8f[:], iota8[:], cols[:, 17:18], None, op0=ALU.is_equal),
                         waits=[("V", p.c("V"))], inc="V")
                    p.op("vector", lambda e: e.tensor_add(cols[:, 14:15], cols[:, 12:13], cols[:, 13:14]),
                         waits=[("V", p.c("V"))], inc="V")
                    p.op("vector", lambda e: e.reciprocal(cols[:, 15:16], cols[:, 14:15]),
                         waits=[("V", p.c("V"))], inc="V")
                    p.op("vector", lambda e: e.tensor_scalar(
                            cols[:, 18:19], cols[:, 12:13], cols[:, 15:16], None, op0=ALU.mult),
                         waits=[("V", p.c("V"))], inc="V")
                    p.op("vector", lambda e: e.tensor_scalar(
                            cols[:, 19:20], cols[:, 13:14], cols[:, 15:16], None, op0=ALU.mult),
                         waits=[("V", p.c("V"))], inc="V")
                    p.op("vector", lambda e: e.tensor_scalar(
                            s8[:], s8[:], cols[:, 18:19], None, op0=ALU.mult),
                         waits=[("V", p.c("V"))], inc="V")
                    p.op("vector", lambda e: e.scalar_tensor_tensor(
                            probs[:], i8f[:], cols[:, 19:20], s8[:], op0=ALU.mult, op1=ALU.add),
                         waits=[("V", p.c("V"))], inc="V")
                    p.op("vector", lambda e, tt=tt: e.tensor_copy(
                            wrall[:, tt * 8:(tt + 1) * 8], probs[:]),
                         waits=[("V", p.c("V"))], inc="V")
                vw = p.c("V")
                for ex in range(8):
                    for tt in range(4):
                        p.op("tensor", lambda e, ex=ex, tt=tt: e.transpose(
                                pH3a[0:1, tt * 128:(tt + 1) * 128],
                                wrall[:, tt * 8 + ex: tt * 8 + ex + 1], ident_f[:]),
                             waits=[("V", vw)] if (ex == 0 and tt == 0) else (),
                             inc="T" if tt == 3 else None)
                    tx = p.c("T")
                    p.op("vector", lambda e: e.tensor_copy(wtmp[0:1, :], pH3a[0:1, 0:512]),
                         waits=[("T", tx)], inc="V")
                    vtx = p.c("V")
                    p.op("tensor", lambda e: e.matmul(
                            pB1[:], ones_bf[:], wtmp[0:1, :], start=True, stop=True),
                         waits=[("V", vtx)], inc="T")
                    tbx = p.c("T")
                    p.op("vector", lambda e, ex=ex: e.tensor_copy(
                            wbc[:, ex * TB:(ex + 1) * TB], pB1[:]),
                         waits=[("T", tbx)], inc="V")
                    p.op("tensor", lambda e: None, waits=[("V", p.c("V"))])
                for (img, dwv, buf) in ((a1tb, a1_done, r1m), (a3tb, a3_done, r3m)):
                    for kc in range(8):
                        st = img[:, kc * 128:(kc + 1) * 128]
                        mv = tfm[:, kc * TB:(kc + 1) * TB]
                        p.op("tensor", lambda e, s=st, m=mv, k=kc: e.matmul(
                                pB1[:], s, m, start=(k == 0), stop=(k == 7)),
                             waits=([("DW", dwv), ("V", p.c("V")), ("S", p.c("S"))]
                                    if kc == 0 else ()),
                             inc="T" if kc == 7 else None)
                    tr = p.c("T")
                    for ex in range(8):
                        p.op("vector", lambda e, ex=ex, b=buf: e.tensor_scalar(
                                b[:, ex * TB:(ex + 1) * TB], pB1[:],
                                emask2[:, ex:ex + 1], None, op0=ALU.mult),
                             waits=[("T", tr)], inc="V")
                    p.op("tensor", lambda e: None, waits=[("V", p.c("V"))])
                p.op("vector", lambda e: e.memset(r2wA[:], 0.0),
                     waits=[("V", p.c("V")), ("T", p.c("T"))], inc="V")
                p.op("vector", lambda e: e.memset(r2wB[:], 0.0), waits=(), inc="V")
                p.op("vector", lambda e: e.memset(r2accA[:], 0.0), waits=(), inc="V")
                p.op("vector", lambda e: e.memset(r2accB[:], 0.0), waits=(), inc="V")
                r2add_prev = p.c("V")
                silu_h = [p.c("S"), p.c("S")]
                m_h = [p.c("V"), p.c("V")]
                r2_h = [p.c("T"), p.c("T")]
                aw_h = [p.c("T"), p.c("T")]
                aw_evac_prev = p.c("V")
                dstate = {"bank_free": p.c("V"), "grp": {}, "bsb": {}}

                def d_load_group(gg):
                    pr = gg % 2
                    wt = [("T", p.c("T"))] if (tb > 0 or gg >= 2) else []
                    dma("sync", actb[:, pr * 5120:(pr + 1) * 5120],
                        actbf[:, gg * 5120:(gg + 1) * 5120], "DS",
                        waits=wt + [("DC", p.c("DC"))])
                    dstate["grp"][gg] = p.c("DS")

                def d_bases(fc):
                    pr = (fc // 2) % 2
                    fo = fc % 2
                    bp = fc % 2
                    for (wo2, bout) in ((0, bsb1), (2048, bsb3)):
                        for kc in range(8):
                            st = actb[:, pr * 5120 + wo2 + kc * 256 + fo * 128:
                                    pr * 5120 + wo2 + kc * 256 + (fo + 1) * 128]
                            mv = tfm[:, kc * TB:(kc + 1) * TB]
                            p.op("tensor", lambda e, s=st, m=mv, k=kc: e.matmul(
                                    pB1[:], s, m, start=(k == 0), stop=(k == 7)),
                                 waits=([("DS", dstate["grp"][fc // 2]),
                                         ("V", dstate["bank_free"])] if kc == 0 else ()),
                                 inc="T" if kc == 7 else None)
                            del st, mv
                        tmm = p.c("T")
                        p.op("vector", lambda e, o=bout, bp=bp: e.tensor_copy(
                                o[:, bp * TB:(bp + 1) * TB], pB1[:]),
                             waits=[("T", tmm)], inc="V")
                        dstate["bank_free"] = p.c("V")
                    dstate["bsb"][fc] = p.c("V")

                dpend = []

                def emit_r2aw(fc2, ex, sl, pr42, fo2, m_v, t2_v):
                    bank = pR2A if ex < 4 else pR2B
                    off = 32 * (ex % 4)
                    p.op("tensor", lambda e, b=bank, off=off, pr4=pr42, fo=fo2, ex=ex,
                                sl=sl: e.matmul(
                            b[off:off + 32, :],
                            actb[:, pr4 * 5120 + 4608 + fo * 256 + 32 * ex:
                                 pr4 * 5120 + 4608 + fo * 256 + 32 * (ex + 1)],
                            mbuf[:, sl * TB:(sl + 1) * TB],
                            start=True, stop=True, tile_position=(0, off)),
                         waits=[("V", m_v), ("V", r2add_prev)], inc="T")
                    r2_h[1] = r2_h[0]
                    r2_h[0] = p.c("T")
                    p.op("tensor", lambda e, sl=sl, ex=ex: e.matmul(
                            pAW[:], ident_bf[:], t2b[:, sl * TB:(sl + 1) * TB],
                            start=(ex == 0), stop=(ex == 7)),
                         waits=([("V", t2_v), ("V", aw_evac_prev)] if ex == 0
                                else [("V", t2_v)]),
                         inc="T")
                    aw_h[1] = aw_h[0]
                    aw_h[0] = p.c("T")

                d_load_group(0)
                d_load_group(1)
                d_bases(0)
                for fc in range(32):
                    pr4 = (fc // 2) % 2
                    fo = fc % 2
                    bpr = fc % 2
                    for ex in range(8):
                        sl = ex % 2
                        hb = pH1a if sl == 0 else pH1b
                        h3b = pH3a if sl == 0 else pH3b
                        if ex == 1 and fc + 1 < 32:
                            d_bases(fc + 1)
                        if ex == 4 and fc % 2 == 0 and fc >= 2 and fc // 2 + 1 < 16:
                            d_load_group(fc // 2 + 1)
                        p.op("tensor", lambda e, hb=hb, pr4=pr4, fo=fo, ex=ex: e.matmul(
                                hb[:], actb[:, pr4 * 5120 + 4096 + fo * 128:
                                            pr4 * 5120 + 4096 + (fo + 1) * 128],
                                r1m[:, ex * TB:(ex + 1) * TB], start=True, stop=False),
                             waits=[("S", silu_h[1]), ("V", dstate["bsb"][fc])], inc=None)
                        p.op("tensor", lambda e, hb=hb, b2=bpr: e.matmul(
                                hb[:], ident_bf[:], bsb1[:, b2 * TB:(b2 + 1) * TB],
                                start=False, stop=True),
                             waits=(), inc="T")
                        h1d = p.c("T")
                        p.op("scalar", lambda e, sl=sl, hb=hb: e.activation(
                                sbuf_s[:, sl * TB:(sl + 1) * TB], hb[:], AF.Silu),
                             waits=[("T", h1d), ("V", m_h[1])], inc="S")
                        silu_h = [p.c("S"), silu_h[0]]
                        p.op("tensor", lambda e, h3b=h3b, pr4=pr4, fo=fo, ex=ex: e.matmul(
                                h3b[:], actb[:, pr4 * 5120 + 4352 + fo * 128:
                                            pr4 * 5120 + 4352 + (fo + 1) * 128],
                                r3m[:, ex * TB:(ex + 1) * TB], start=True, stop=False),
                             waits=[("V", m_h[1])], inc=None)
                        p.op("tensor", lambda e, h3b=h3b, b2=bpr: e.matmul(
                                h3b[:], ident_bf[:], bsb3[:, b2 * TB:(b2 + 1) * TB],
                                start=False, stop=True),
                             waits=(), inc="T")
                        h3d = p.c("T")
                        p.op("vector", lambda e, sl=sl, h3b=h3b: e.tensor_mul(
                                mbuf[:, sl * TB:(sl + 1) * TB],
                                sbuf_s[:, sl * TB:(sl + 1) * TB], h3b[:]),
                             waits=[("S", silu_h[0]), ("T", h3d), ("T", r2_h[1])],
                             inc="V")
                        m_prev = p.c("V")
                        m_h = [m_prev, m_h[0]]
                        p.op("vector", lambda e, sl=sl, ex=ex: e.tensor_mul(
                                t2b[:, sl * TB:(sl + 1) * TB],
                                mbuf[:, sl * TB:(sl + 1) * TB], wbc[:, ex * TB:(ex + 1) * TB]),
                             waits=[("T", aw_h[1])], inc="V")
                        dpend.append((fc, ex, sl, pr4, fo, m_prev, p.c("V")))
                        if len(dpend) > 1:
                            emit_r2aw(*dpend.pop(0))
                    if dpend:
                        emit_r2aw(*dpend.pop(0))
                    p.op("vector", lambda e: e.tensor_add(r2accA[:], r2accA[:], pR2A[:]),
                         waits=[("T", r2_h[0])], inc="V")
                    p.op("vector", lambda e: e.tensor_add(r2accB[:], r2accB[:], pR2B[:]),
                         waits=(), inc="V")
                    r2add_prev = p.c("V")
                    p.op("vector", lambda e, fc=fc: e.tensor_copy(
                            actw[:, fc * 512:(fc + 1) * 512], pAW[:]),
                         waits=[("T", aw_h[0])], inc="V")
                    aw_evac_prev = p.c("V")
                for ex in range(8):
                    bank = r2accA if ex < 4 else r2accB
                    dst = r2wA if ex < 4 else r2wB
                    off = 32 * (ex % 4)
                    p.op("vector", lambda e, b=bank, d=dst, off=off, ex=ex: e.scalar_tensor_tensor(
                            d[off:off + 16, :], b[off:off + 16, :], 2.0,
                            wbc[off:off + 16, ex * TB:(ex + 1) * TB],
                            op0=ALU.mult, op1=ALU.mult),
                         waits=[("V", r2add_prev)], inc="V")
                r2w_ready = p.c("V")
                aw_all = p.c("V")
                if debug_stage == "d1" and tb == 0:
                    p.op("sync", lambda e: e.dma_start(out=dbg["hnorm"][:, 0:4096], in_=tfm[:]),
                         waits=[("V", aw_all)], inc="DO", dma=True)
                    p.op("sync", lambda e: e.dma_start(out=dbg["hnorm"][:, 4096:8192], in_=wbc[:]),
                         waits=[("V", aw_all)], inc="DO", dma=True)
                    p.op("sync", lambda e: e.dma_start(out=dbg["hnorm"][:, 8192:8704], in_=r2wA[:]),
                         waits=[("V", aw_all)], inc="DO", dma=True)
                    p.op("sync", lambda e: e.dma_start(out=dbg["hnorm"][:, 8704:9216], in_=r2wB[:]),
                         waits=[("V", aw_all)], inc="DO", dma=True)
                    p.op("sync", lambda e: e.dma_start(out=dbg["g"][:], in_=actw[:]),
                         waits=[("V", aw_all)], inc="DO", dma=True)
                    p.op("sync", lambda e: None, waits=[("DO", p.c("DO"))])
                    p.flush_block()
                    return nc
                dbanks = (pB1, pH1a, pH1b, pH3a, pH3b, pAW, pR2A, pR2B)
                w2rdy = {}
                w2last = {}

                def d_load_w2(g4):
                    prw = g4 % 2
                    wt = ([("T", w2last[g4 - 2])] if (g4 - 2) in w2last
                          else ([("T", p.c("T"))] if tb > 0 else []))
                    dma("scalar", w2b[:, prw * 4096:(prw + 1) * 4096],
                        w2bf[:, g4 * 4096:(g4 + 1) * 4096], "DT",
                        waits=wt + [("DC", p.c("DC"))])
                    w2rdy[g4] = p.c("DT")

                d_load_w2(0)
                d_load_w2(1)
                for fc in range(32):
                    g2 = fc // 4
                    if fc % 4 == 0 and g2 >= 1 and g2 + 1 < 8:
                        d_load_w2(g2 + 1)
                    prw = g2 % 2
                    wcol = w2b[:, prw * 4096 + (fc % 4) * 1024: prw * 4096 + (fc % 4 + 1) * 1024]
                    for tt in range(4):
                        for half in range(2):
                            bank = dbanks[tt * 2 + half]
                            st = actw[:, fc * 512 + tt * 128: fc * 512 + (tt + 1) * 128]
                            mv = wcol[:, half * 512:(half + 1) * 512]
                            p.op("tensor", lambda e, b=bank, s=st, m=mv, fc=fc: e.matmul(
                                    b[:], s, m, start=(fc == 0), stop=False),
                                 waits=([("DT", w2rdy[g2]), ("V", aw_all), ("S", p.c("S"))]
                                        if (tt == 0 and half == 0) else ()),
                                 inc="T" if (tt == 3 and half == 1) else None)
                    w2last[g2] = p.c("T")
                for tt in range(4):
                    t = tb * 4 + tt
                    for half in range(2):
                        bank = dbanks[tt * 2 + half]
                        p.op("tensor", lambda e, b=bank, tt=tt, h2=half: e.matmul(
                                b[:], r2wA[:, tt * 128:(tt + 1) * 128],
                                b2tab[:, h2 * 512:(h2 + 1) * 512], start=False, stop=False),
                             waits=([("V", r2w_ready), ("DW", b2a_done)]
                                    if (tt == 0 and half == 0) else ()),
                             inc=None)
                        p.op("tensor", lambda e, b=bank, tt=tt, h2=half: e.matmul(
                                b[:], r2wB[:, tt * 128:(tt + 1) * 128],
                                b2tbb[:, h2 * 512:(h2 + 1) * 512], start=False, stop=True),
                             waits=([("DW", b2b_done)] if (tt == 0 and half == 0) else ()),
                             inc="T")
                    tdn = p.c("T")
                    for half in range(2):
                        bank = dbanks[tt * 2 + half]
                        p.op("vector", lambda e, b=bank, h2=half: e.tensor_copy(
                                tmp1[:, h2 * 512:(h2 + 1) * 512], b[:]),
                             waits=[("T", tdn), ("S", p.c("S")), ("V", p.c("V"))], inc="V")
                    ln_stats(tmp1, EPS_LN, [("V", p.c("V"))])
                    p.op("vector", lambda e: e.tensor_scalar(
                            tmp1, tmp1, cols[:, 0:1], cols[:, 2:3],
                            op0=ALU.subtract, op1=ALU.mult),
                         waits=[("V", p.c("V"))], inc="V")
                    yt = stream[:, t * D:(t + 1) * D]
                    p.op("vector", lambda e, yt=yt: e.tensor_add(yt, tmp1, yt),
                         waits=[("V", p.c("V"))], inc="V")
                    p.op("sync", lambda e, t=t, yt=yt: e.dma_start(
                            out=y_out[t * 128:(t + 1) * 128, :], in_=yt),
                         waits=[("V", p.c("V"))], inc="DO", dma=True)
            p.op("sync", lambda e: None, waits=[("DO", p.c("DO"))])
            p.flush_block()

    return nc



def _prep_inputs(inputs):
    g = {k: np.asarray(v) for k, v in inputs.items()}
    out = {}

    def img8(w):
        M = w.shape[1]
        return np.ascontiguousarray(w.reshape(8, 128, M).transpose(1, 0, 2).reshape(128, 8 * M))

    def img32(w):
        M = w.shape[1]
        return np.ascontiguousarray(w.reshape(32, 128, M).transpose(1, 0, 2).reshape(128, 32 * M))

    for name, w in (("wq_h", g["L0_attn_wq"]), ("wk_h", g["L0_attn_wk"]), ("wv_h", g["L0_attn_wv"])):
        out[name] = np.ascontiguousarray(
            w.reshape(8, 128, 4, 256).transpose(2, 1, 0, 3).reshape(4, 128, 8 * 256))
    out["wo_img"] = img8(g["L0_attn_wo"])
    out["wff1_img"] = img8(g["L0_attn_w_ff1"])
    out["wff2_img"] = img32(g["L0_attn_w_ff2"])
    out["wd_img"] = img8(g["L0_attn_w_d"])
    out["w1t_img"] = img8(np.ascontiguousarray(g["L0_ffn_w1"].T))
    out["w3t_img"] = img8(np.ascontiguousarray(g["L0_ffn_w3"].T))
    out["w2t_img"] = img32(np.ascontiguousarray(g["L0_ffn_w2"].T))
    out["gate_img"] = img8(np.ascontiguousarray(g["L0_ffn_gate_w"].T))
    out["a1t_img"] = img8(np.ascontiguousarray(
        g["L0_ffn_a1"].transpose(2, 0, 1).reshape(1024, 128)))
    out["a3t_img"] = img8(np.ascontiguousarray(
        g["L0_ffn_a3"].transpose(2, 0, 1).reshape(1024, 128)))
    out["b1t_img"] = np.ascontiguousarray(g["L0_ffn_b1"].transpose(0, 2, 1).reshape(128, 4096))
    out["b3t_img"] = np.ascontiguousarray(g["L0_ffn_b3"].transpose(0, 2, 1).reshape(128, 4096))
    a2p = np.zeros((4096, 256), np.float32)
    for e in range(8):
        a2p[:, 32 * e:32 * e + 16] = g["L0_ffn_a2"][e].T
    out["a2t_img"] = img32(a2p)
    blocks = []
    for gg in range(16):
        blocks.extend([out["w1t_img"][:, kc * FF + gg * 256: kc * FF + (gg + 1) * 256]
                       for kc in range(8)])
        blocks.extend([out["w3t_img"][:, kc * FF + gg * 256: kc * FF + (gg + 1) * 256]
                       for kc in range(8)])
        blocks.append(out["b1t_img"][:, gg * 256:(gg + 1) * 256])
        blocks.append(out["b3t_img"][:, gg * 256:(gg + 1) * 256])
        blocks.append(out["a2t_img"][:, gg * 512:(gg + 1) * 512])
    out["actstream"] = np.ascontiguousarray(np.concatenate(blocks, axis=1))
    b2t = g["L0_ffn_b2"].transpose(0, 2, 1)
    b2a = np.zeros((128, 1024), np.float32)
    b2b = np.zeros((128, 1024), np.float32)
    for e in range(4):
        b2a[32 * e:32 * e + 16] = b2t[e]
        b2b[32 * e:32 * e + 16] = b2t[e + 4]
    out["b2ta_img"] = b2a
    out["b2tb_img"] = b2b
    tile = lambda v: np.ascontiguousarray(np.tile(np.asarray(v, np.float32)[None, :], (128, 1)))
    out["ln1g_t"] = tile(g["L0_attn_ln1_g"]); out["ln1b_t"] = tile(g["L0_attn_ln1_b"])
    out["ln2g_t"] = tile(g["L0_attn_ln2_g"]); out["ln2b_t"] = tile(g["L0_attn_ln2_b"])
    out["bo_t"] = tile(g["L0_attn_bo"]); out["bff2_t"] = tile(g["L0_attn_b_ff2"])
    out["bd_t"] = tile(g["L0_attn_b_d"]); out["nw_t"] = tile(g["L0_ffn_norm_w"])
    out["bff1_fm"] = np.ascontiguousarray(g["L0_attn_b_ff1"].reshape(32, 128).T)
    out["ident_bf"] = np.eye(128, dtype=bf)
    out["ident_f"] = np.eye(128, dtype=np.float32)
    out["ones_bf"] = np.ones((1, 128), dtype=bf)
    out["iota8"] = np.ascontiguousarray(
        np.tile(np.arange(8, dtype=np.float32)[None, :], (128, 1)))
    out["nine_m"] = np.ascontiguousarray(
        9.0 - np.tile(np.arange(8, dtype=np.float32)[None, :], (128, 1)))
    ep = np.zeros((128, 2), np.float32)
    ep[:, 0] = EPS_LN
    ep[:, 1] = EPS_RMS
    out["epsc"] = ep
    em = np.zeros((128, 8), np.float32)
    for e in range(8):
        em[16 * e:16 * e + 16, e] = 2.0
    out["emask2"] = em
    return out


def kernel(**inputs):
    shared = _prep_inputs(inputs)
    x = np.asarray(inputs["L0_attn_x"], np.float32)
    nc = build(DEBUG_STAGE)
    in_maps = []
    for c in range(8):
        b, h = c // 2, c % 2
        xc = np.ascontiguousarray(
            np.concatenate([x[b, h * TOK:(h + 1) * TOK], x[b, (1 - h) * TOK:(2 - h) * TOK]],
                           axis=0))
        m = dict(shared)
        m["x_core"] = xc
        in_maps.append(m)
    res = run_bass_kernel_spmd(nc, in_maps, core_ids=list(range(8)))
    out = np.empty((B, N, D), np.float32)
    for c in range(8):
        b, h = c // 2, c % 2
        out[b, h * TOK:(h + 1) * TOK, :] = res.results[c]["y"]
    return out


# version 23
# speedup vs baseline: 1.0480x; 1.0480x over previous
import numpy as np
import ml_dtypes
from contextlib import ExitStack

import concourse.bass as bass
import concourse.mybir as mybir
from concourse.bass_utils import run_bass_kernel_spmd

F32 = mybir.dt.float32
BF16 = mybir.dt.bfloat16
U32 = mybir.dt.uint32
AF = mybir.ActivationFunctionType
ALU = mybir.AluOpType
AX = mybir.AxisListType
bf = ml_dtypes.bfloat16

B, N, D = 4, 4096, 1024
HEADS, DH = 4, 256
FF = 4096
TOK = 2048
NT = TOK // 128
NTF = N // 128
TB = 512
NTB = TOK // TB
EPS_LN = 1e-5
EPS_RMS = 1e-6
SCALE_QK = float(DH ** (-0.25))

DEBUG_STAGE = None

ENGINES = ("sync", "gpsimd", "tensor", "vector", "scalar")


class Prog:

    def __init__(self, nc, es):
        self.nc = nc
        self.sems = {}
        self.cnt = {}
        for name in ("DW", "DX", "DS", "DT", "DC", "T", "V", "S", "DO"):
            self.sems[name] = es.enter_context(nc.semaphore(name))
            self.cnt[name] = 0
        self.items = {e: [] for e in ENGINES}

    def op(self, eng, builder, waits=(), inc=None, dma=False):
        n = 16 if dma else 1
        fixed = []
        for (s, v) in waits:
            if s in ("DW", "DX", "DS", "DT", "DC", "DO") and 0 < v:
                v = self.cnt[s]
            fixed.append((s, v))
        if dma and inc is not None and self.cnt[inc] > 0 and inc != "DC":
            fixed.append((inc, self.cnt[inc]))
        val = None
        if inc is not None:
            self.cnt[inc] += n
            val = self.cnt[inc]
        self.items[eng].append((builder, tuple(fixed), inc, n))
        return val

    def c(self, sem):
        return self.cnt[sem]

    def flush_block(self):
        items = self.items
        self.items = {e: [] for e in ENGINES}
        sems = self.sems
        with self.nc.Block() as blk:
            def mk(ename):
                def body(eng):
                    last_wait = {}
                    for builder, waits, inc, n in items[ename]:
                        for (s, v) in waits:
                            if v > 0 and last_wait.get(s, -1) < v:
                                eng.wait_ge(sems[s], v)
                                last_wait[s] = v
                        inst = builder(eng)
                        if inc is not None:
                            inst.then_inc(sems[inc], n)
                return body
            for ename in ENGINES:
                if items[ename]:
                    getattr(blk, ename)(mk(ename))


def build(debug_stage=None):
    nc = bass.Bass("TRN2", target_bir_lowering=False)

    dr = {}
    def din(name, shape, dt=F32):
        dr[name] = nc.dram_tensor(name, list(shape), dt, kind="ExternalInput")
        return dr[name]

    x_core = din("x_core", (N, D))
    wq_h = din("wq_h", (HEADS, 128, 8 * DH))
    wk_h = din("wk_h", (HEADS, 128, 8 * DH))
    wv_h = din("wv_h", (HEADS, 128, 8 * DH))
    wo_img = din("wo_img", (128, 8 * 1024))
    wff1_img = din("wff1_img", (128, 8 * FF))
    wff2_img = din("wff2_img", (128, 32 * 1024))
    wd_img = din("wd_img", (128, 8 * 1024))
    w1t_img = din("w1t_img", (128, 8 * FF))
    w3t_img = din("w3t_img", (128, 8 * FF))
    w2t_img = din("w2t_img", (128, 32 * 1024))
    gate_img = din("gate_img", (128, 8 * 8))
    a1t_img = din("a1t_img", (128, 8 * 128))
    a3t_img = din("a3t_img", (128, 8 * 128))
    b1t_img = din("b1t_img", (128, FF))
    b3t_img = din("b3t_img", (128, FF))
    a2t_img = din("a2t_img", (128, 32 * 256))
    b2ta_img = din("b2ta_img", (128, 1024))
    b2tb_img = din("b2tb_img", (128, 1024))
    actstream = din("actstream", (128, 16 * 5120))
    actbf = nc.dram_tensor("actbf", [128, 16 * 5120], BF16)
    w2bf = nc.dram_tensor("w2bf", [128, 32 * 1024], BF16)
    wff1bf = nc.dram_tensor("wff1bf", [128, 8 * FF], BF16)
    wff2bf = nc.dram_tensor("wff2bf", [128, 32 * 1024], BF16)
    ln1g_t = din("ln1g_t", (128, D))
    ln1b_t = din("ln1b_t", (128, D))
    ln2g_t = din("ln2g_t", (128, D))
    ln2b_t = din("ln2b_t", (128, D))
    bo_t = din("bo_t", (128, D))
    bff2_t = din("bff2_t", (128, D))
    bd_t = din("bd_t", (128, D))
    nw_t = din("nw_t", (128, D))
    bff1_fm_d = din("bff1_fm", (128, 32))
    ident_bf_d = din("ident_bf", (128, 128), BF16)
    ident_f_d = din("ident_f", (128, 128), F32)
    ones_bf_d = din("ones_bf", (1, 128), BF16)
    iota8_d = din("iota8", (128, 8))
    nine_m_d = din("nine_m", (128, 8))
    emask2_d = din("emask2", (128, 8))
    epsc_d = din("epsc", (128, 2))

    y_out = nc.dram_tensor("y", [TOK, D], F32, kind="ExternalOutput")
    dbg = {}
    if debug_stage is not None:
        dbg["stream"] = nc.dram_tensor("dbg_stream", [128, NT * D], F32, kind="ExternalOutput")
        dbg["hnorm"] = nc.dram_tensor("dbg_hnorm", [128, 8 * N], BF16, kind="ExternalOutput")
        dbg["ctx"] = nc.dram_tensor("dbg_ctx", [128, 8 * DH], BF16, kind="ExternalOutput")
        dbg["g"] = nc.dram_tensor("dbg_g", [128, 32 * 512], BF16, kind="ExternalOutput")

    es = ExitStack()
    with es:
        p = Prog(nc, es)
        sbT = lambda name, shape, dt=F32: es.enter_context(nc.sbuf_tensor(name, list(shape), dt))

        stream = sbT("stream", (128, NT * D))
        tmp1 = sbT("tmp1", (128, D))[:]
        tbf1 = sbT("tbf1", (128, D), BF16)[:]
        cols = sbT("cols", (128, 32))
        bn12 = sbT("bn12", (128, 12))
        ident_bf = sbT("ident_bf_s", (128, 128), BF16)
        ident_f = sbT("ident_f_s", (128, 128), F32)
        ones_bf = sbT("ones_bf_s", (1, 128), BF16)
        iota8 = sbT("iota8_s", (128, 8))
        nine_m = sbT("nine_m_s", (128, 8))
        emask2 = sbT("emask2_s", (128, 8))
        bff1_fm = sbT("bff1_fm_s", (128, 32))
        epsc = sbT("epsc_s", (128, 2))
        pva = sbT("pva", (128, D))
        es_bc = ExitStack()
        pvb = es_bc.enter_context(nc.sbuf_tensor("pvb", [128, D], F32))
        pvc = es_bc.enter_context(nc.sbuf_tensor("pvc", [128, D], F32))
        es.enter_context(es_bc)

        def dma(eng, dst, src, sem, waits=()):
            return p.op(eng, lambda e, d=dst, s=src: e.dma_start(out=d, in_=s),
                        waits=waits, inc=sem, dma=True)

        def ln_stats(x_ap, eps, waits=()):
            for j in range(2):
                p.op("vector", lambda e, j=j, x=x_ap: e.bn_stats(
                        bn12[:, 6 * j:6 * j + 6], x[:, 512 * j:512 * (j + 1)]),
                     waits=waits, inc="V")
            p.op("vector", lambda e: e.bn_aggr(cols[:, 0:2], bn12[:, 0:12]),
                 waits=[("V", p.c("V"))], inc="V")
            ec = 0 if eps == EPS_LN else 1
            p.op("scalar", lambda e, ec=ec: e.activation(
                    cols[:, 3:4], cols[:, 1:2], AF.Sqrt, bias=epsc[:, ec:ec + 1]),
                 waits=[("V", p.c("V"))], inc="S")
            p.op("vector", lambda e: e.reciprocal(cols[:, 2:3], cols[:, 3:4]),
                 waits=[("S", p.c("S"))], inc="V")

        def rms_stats(x_ap, eps, waits=()):
            for j in range(2):
                p.op("vector", lambda e, j=j, x=x_ap: e.bn_stats(
                        bn12[:, 6 * j:6 * j + 6], x[:, 512 * j:512 * (j + 1)]),
                     waits=waits, inc="V")
            p.op("vector", lambda e: e.bn_aggr(cols[:, 0:2], bn12[:, 0:12]),
                 waits=[("V", p.c("V"))], inc="V")
            p.op("vector", lambda e: e.scalar_tensor_tensor(
                    cols[:, 4:5], cols[:, 0:1], cols[:, 0:1], cols[:, 1:2],
                    op0=ALU.mult, op1=ALU.add),
                 waits=[("V", p.c("V"))], inc="V")
            ec = 0 if eps == EPS_LN else 1
            p.op("scalar", lambda e, ec=ec: e.activation(
                    cols[:, 3:4], cols[:, 4:5], AF.Sqrt, bias=epsc[:, ec:ec + 1]),
                 waits=[("V", p.c("V"))], inc="S")
            p.op("vector", lambda e: e.reciprocal(cols[:, 2:3], cols[:, 3:4]),
                 waits=[("S", p.c("S"))], inc="V")

        def cast_transpose_store(src_f32, dst, dst_off, dst_stride, pt_bank, extra_waits=()):
            p.op("scalar", lambda e, s=src_f32: e.activation(tbf1[:], s, AF.Copy),
                 waits=list(extra_waits) + [("T", p.c("T"))], inc="S")
            sh = p.c("S")
            for j in range(8):
                p.op("tensor", lambda e, j=j, b=pt_bank: e.transpose(
                        b[:, j * 128:(j + 1) * 128], tbf1[:, j * 128:(j + 1) * 128], ident_bf[:]),
                     waits=[("S", sh)] if j == 0 else (), inc="T" if j == 7 else None)
            th = p.c("T")
            p.op("vector", lambda e, b=pt_bank: e.tensor_copy(
                    bass.AP(dst, dst_off, [[dst.shape[1], 128], [dst_stride, 8], [1, 128]]),
                    b[:]),
                 waits=[("T", th)], inc="V")
            p.op("tensor", lambda e: None, waits=[("V", p.c("V"))])

        dma("sync", ident_bf[:], ident_bf_d[:], "DX")
        dma("sync", ident_f[:], ident_f_d[:], "DX")
        dma("sync", ones_bf[:], ones_bf_d[:], "DX")
        dma("sync", iota8[:], iota8_d[:], "DX")
        dma("sync", nine_m[:], nine_m_d[:], "DX")
        dma("sync", emask2[:], emask2_d[:], "DX")
        dma("sync", epsc[:], epsc_d[:], "DX")
        dma("sync", bff1_fm[:], bff1_fm_d[:], "DX")
        dma("sync", pva[:], ln1g_t[:], "DX")
        dma("sync", pvb[:], ln1b_t[:], "DX")
        init_consts = p.c("DX")
        p.op("sync", lambda e: None, waits=[("DX", init_consts)])
        p.flush_block()

        esA0 = ExitStack()
        es.enter_context(esA0)
        h_norm = esA0.enter_context(nc.sbuf_tensor("h_norm", [128, 8 * N], BF16))
        if True:
            ctx = esA0.enter_context(nc.sbuf_tensor("ctx", [128, 8 * DH], BF16))
            if True:
                esA1 = ExitStack()
                esA0.enter_context(esA1)
                _a = esA1.enter_context
                xs = _a(nc.sbuf_tensor("xs", [128, D], F32))
                wkh = _a(nc.sbuf_tensor("wkh", [128, 8 * DH], BF16))
                wvh = _a(nc.sbuf_tensor("wvh", [128, 8 * DH], BF16))
                kraw = _a(nc.sbuf_tensor("kraw", [128, N], F32))
                expbf = _a(nc.sbuf_tensor("expbf", [128, N], BF16))
                ktm = _a(nc.sbuf_tensor("ktm", [128, NTF * DH], BF16))
                vbf = _a(nc.sbuf_tensor("vbf", [128, 4 * DH], BF16))
                rsk = _a(nc.sbuf_tensor("rsk", [128, 8], F32))
                pa = _a(nc.psum_tensor("pa", [128, 512], F32))
                pb = _a(nc.psum_tensor("pb", [128, 512], F32))
                pc = _a(nc.psum_tensor("pc", [128, 512], F32))
                pd = _a(nc.psum_tensor("pd", [128, 512], F32))
                pe2 = _a(nc.psum_tensor("pe2", [128, 512], F32))
                pt1 = _a(nc.psum_tensor("pt1", [128, 1024], BF16))
                pt1c = _a(nc.psum_tensor("pt1c", [128, 1024], BF16))
                if True:

                    pt1b = pt1
                    wk_done0 = dma("gpsimd", wkh[:], wk_h[0], "DW")
                    wv_done0 = dma("gpsimd", wvh[:], wv_h[0], "DW")
                    kc0_evacs = 0
                    hn_done = {}

                    def s0_stats_a(t):
                        par = t % 2
                        cb = par * 16
                        xbuf = xs[:]
                        wx = dma("sync", xbuf, x_core[t * 128:(t + 1) * 128, :], "DX",
                                 waits=[("V", p.c("V"))])
                        for jj in range(2):
                            p.op("vector", lambda e, jj=jj, x=xbuf: e.bn_stats(
                                    bn12[:, 6 * jj:6 * jj + 6], x[:, 512 * jj:512 * (jj + 1)]),
                                 waits=[("DX", wx)], inc="V")
                        p.op("vector", lambda e, cb=cb: e.bn_aggr(cols[:, cb:cb + 2], bn12[:, 0:12]),
                             waits=[("V", p.c("V"))], inc="V")
                        p.op("scalar", lambda e, cb=cb: e.activation(
                                cols[:, cb + 3:cb + 4], cols[:, cb + 1:cb + 2], AF.Sqrt,
                                bias=epsc[:, 0:1]),
                             waits=[("V", p.c("V"))], inc="S")

                    def s0_stats_b(t):
                        par = t % 2
                        cb = par * 16
                        xbuf = xs[:]
                        p.op("vector", lambda e, cb=cb: e.reciprocal(
                                cols[:, cb + 2:cb + 3], cols[:, cb + 3:cb + 4]),
                             waits=[("S", p.c("S"))], inc="V")
                        if t < NT:
                            u_dst = stream[:, t * D:(t + 1) * D]
                        else:
                            u_dst = expbf[:, par * 1024:(par + 1) * 1024]
                        p.op("vector", lambda e, xb=xbuf, ud=u_dst, cb=cb: e.tensor_scalar(
                                ud, xb, cols[:, cb:cb + 1], cols[:, cb + 2:cb + 3],
                                op0=ALU.subtract, op1=ALU.mult),
                             waits=[("V", p.c("V")), ("S", p.c("S")), ("T", p.c("T"))], inc="V")
                        p.op("vector", lambda e, cb=cb: e.scalar_tensor_tensor(
                                cols[:, cb + 4:cb + 5], cols[:, cb + 2:cb + 3],
                                cols[:, cb + 2:cb + 3], cols[:, cb + 1:cb + 2],
                                op0=ALU.mult, op1=ALU.mult),
                             waits=[("V", p.c("V"))], inc="V")
                        p.op("scalar", lambda e, cb=cb: e.activation(
                                cols[:, cb + 5:cb + 6], cols[:, cb + 4:cb + 5], AF.Sqrt,
                                bias=epsc[:, 0:1]),
                             waits=[("V", p.c("V"))], inc="S")
                        return u_dst

                    def s0_recip2(t):
                        cb = (t % 2) * 16
                        p.op("vector", lambda e, cb=cb: e.reciprocal(
                                cols[:, cb + 6:cb + 7], cols[:, cb + 5:cb + 6]),
                             waits=[("S", p.c("S"))], inc="V")

                    def s0_apply_a(t, u_src):
                        cb = (t % 2) * 16
                        p.op("vector", lambda e, us=u_src, cb=cb: e.tensor_scalar(
                                tmp1, us, cols[:, cb + 6:cb + 7], None, op0=ALU.mult),
                             waits=[("V", p.c("V")), ("S", p.c("S"))], inc="V")
                        p.op("vector", lambda e: e.tensor_mul(tmp1, tmp1, pva[:]),
                             waits=[("V", p.c("V"))], inc="V")

                    def s0_apply_b(t):
                        p.op("vector", lambda e: e.tensor_add(tmp1, tmp1, pvb[:]),
                             waits=[("V", p.c("V"))], inc="V")
                        cast_transpose_store(tmp1, h_norm, t * 128, N, pt1,
                                             extra_waits=[("V", p.c("V"))])
                        hn_done[t] = p.c("V")
                        if t % 4 == 3:
                            nb = t // 4
                            bank = pa if nb % 2 == 0 else pb
                            for kc in range(8):
                                st = wkh[:, kc * DH: kc * DH + 128]
                                mv = h_norm[:, kc * N + nb * 512: kc * N + (nb + 1) * 512]
                                p.op("tensor", lambda e, b=bank, s=st, m=mv, k=kc: e.matmul(
                                        b[:], s, m, start=(k == 0), stop=(k == 7)),
                                     waits=([("DW", wk_done0), ("V", hn_done[t])] if kc == 0 else ()),
                                     inc="T" if kc == 7 else None)
                            tmm = p.c("T")
                            p.op("vector", lambda e, b=bank, nb=nb: e.tensor_copy(
                                    kraw[:, nb * 512:(nb + 1) * 512], b[:]),
                                 waits=[("T", tmm), ("S", p.c("S"))], inc="V")
                            nonlocal_kc0[0] = p.c("V")
                    nonlocal_kc0 = [0]
                    prev = None
                    for t in range(NTF):
                        s0_stats_a(t)
                        if prev is not None:
                            s0_apply_a(*prev)
                        u = s0_stats_b(t)
                        if prev is not None:
                            s0_apply_b(prev[0])
                        s0_recip2(t)
                        prev = (t, u)
                    s0_apply_a(*prev)
                    s0_apply_b(prev[0])
                    kc0_evacs = nonlocal_kc0[0]

                    if debug_stage == "s0":
                        p.op("sync", lambda e: e.dma_start(out=dbg["stream"][:], in_=stream[:]),
                             waits=[("V", p.c("V"))], inc="DO", dma=True)
                        p.op("sync", lambda e: e.dma_start(out=dbg["hnorm"][:], in_=h_norm[:]),
                             waits=[("V", p.c("V"))], inc="DO", dma=True)
                        p.op("sync", lambda e: None, waits=[("DO", p.c("DO"))])
                        p.flush_block()
                        return nc

                    kbanks = (pa, pb, pc)
                    kev = {0: kc0_evacs, 1: kc0_evacs, 2: kc0_evacs}
                    exp_guard = [0]
                    wk_free = [0]
                    wv_free = [0]

                    def k_mms(c, wk_done, eg):
                        for nb in range(8):
                            bi = nb % 3
                            bank = kbanks[bi]
                            for kc in range(8):
                                st = wkh[:, kc * DH + c * 128: kc * DH + (c + 1) * 128]
                                mv = h_norm[:, kc * N + nb * 512: kc * N + (nb + 1) * 512]
                                p.op("tensor", lambda e, b=bank, s=st, m=mv, k=kc: e.matmul(
                                        b[:], s, m, start=(k == 0), stop=(k == 7)),
                                     waits=([("DW", wk_done), ("V", kev[bi])] if kc == 0 else ()),
                                     inc="T" if kc == 7 else None)
                            tmm = p.c("T")
                            p.op("vector", lambda e, b=bank, nb=nb: e.tensor_copy(
                                    kraw[:, nb * 512:(nb + 1) * 512], b[:]),
                                 waits=[("T", tmm), ("S", eg)], inc="V")
                            kev[bi] = p.c("V")

                    def k_softmax(c, h, vk):
                        p.op("vector", lambda e: e.reduce_max(
                                cols[:, 8:9], kraw[:], axis=AX.X, negate=True),
                             waits=[("V", vk)], inc="V")
                        p.op("vector", lambda e: e.tensor_scalar(
                                cols[:, 9:10], cols[:, 8:9], SCALE_QK, None, op0=ALU.mult),
                             waits=[("V", p.c("V"))], inc="V")
                        vb2 = p.c("V")
                        p.op("scalar", lambda e: e.activation(
                                expbf[:], kraw[:], AF.Exp, bias=cols[:, 9:10], scale=SCALE_QK,
                                accum_out=cols[:, 10:11]),
                             waits=[("V", vb2), ("T", p.c("T"))], inc="S")
                        exp_guard[0] = p.c("S")
                        p.op("vector", lambda e, c=c, h=h: e.reciprocal(
                                rsk[:, h * 2 + c: h * 2 + c + 1], cols[:, 10:11]),
                             waits=[("S", p.c("S"))], inc="V")

                    def k_trans(c):
                        sexp = p.c("S")
                        tpend = []
                        tev = {}

                        def t_evac(tg2, bank2, tt2):
                            p.op("vector", lambda e, b=bank2, tg=tg2, c=c: e.tensor_copy(
                                    bass.AP(ktm, tg * 8 * DH + c * 128,
                                            [[NTF * DH, 128], [DH, 8], [1, 128]]),
                                    b[:]),
                                 waits=[("T", tt2)], inc="V")
                            tev[tg2] = p.c("V")

                        for tg in range(4):
                            bank = pt1 if tg % 2 == 0 else pt1c
                            for j in range(8):
                                t = tg * 8 + j
                                w0 = []
                                if j == 0:
                                    if tg == 0:
                                        w0.append(("S", sexp))
                                    if tg >= 2:
                                        w0.append(("V", tev[tg - 2]))
                                p.op("tensor", lambda e, t=t, j=j, b=bank: e.transpose(
                                        b[:, j * 128:(j + 1) * 128],
                                        expbf[:, t * 128:(t + 1) * 128], ident_bf[:]),
                                     waits=w0, inc="T" if j == 7 else None)
                            tpend.append((tg, bank, p.c("T")))
                            if len(tpend) > 1:
                                t_evac(*tpend.pop(0))
                        while tpend:
                            t_evac(*tpend.pop(0))

                    for h in range(HEADS):
                        if h > 0:
                            wk_done = dma("gpsimd", wkh[:], wk_h[h], "DW",
                                          waits=[("T", wk_free[0])])
                            wv_done = dma("gpsimd", wvh[:], wv_h[h], "DW",
                                          waits=[("T", wv_free[0])])
                        else:
                            wk_done, wv_done = wk_done0, wv_done0
                        if h == 1:
                            for (dstp, srcp) in ((wff1bf, wff1_img), (actbf, actstream),
                                                 (w2bf, w2t_img), (wff2bf, wff2_img)):
                                npc = dstp.shape[1]
                                for q in range(4):
                                    dma("gpsimd", dstp[:, q * npc // 4:(q + 1) * npc // 4],
                                        srcp[:, q * npc // 4:(q + 1) * npc // 4], "DC")
                        if h > 0:
                            k_mms(0, wk_done, exp_guard[0])
                            vk0 = p.c("V")
                        else:
                            vk0 = kc0_evacs
                        k_softmax(0, h, vk0)
                        k_mms(1, wk_done, exp_guard[0])
                        wk_free[0] = p.c("T")
                        vk1 = p.c("V")
                        k_trans(0)
                        k_softmax(1, h, vk1)
                        vfree = {0: kev[0], 1: kev[1]}
                        pend = []
                        ctx_t = {}
                        kt_v = [0]

                        def v_mm(t):
                            bank = pa if t % 2 == 0 else pb
                            bi = t % 2
                            for kc in range(8):
                                st = h_norm[:, kc * N + t * 128: kc * N + (t + 1) * 128]
                                mv = wvh[:, kc * DH:(kc + 1) * DH]
                                p.op("tensor", lambda e, b=bank, s=st, m=mv, k=kc: e.matmul(
                                        b[:, 0:DH], s, m, start=(k == 0), stop=(k == 7)),
                                     waits=([("DW", wv_done), ("V", vfree[bi])]
                                            if kc == 0 else ()),
                                     inc="T" if kc == 7 else None)
                            tv = p.c("T")
                            vb = vbf[:, (t % 4) * DH:(t % 4 + 1) * DH]
                            p.op("vector", lambda e, vb=vb, b=bank: e.tensor_copy(
                                    vb, b[:, 0:DH]),
                                 waits=[("T", max(tv, ctx_t.get(t - 4, 0)))], inc="V")
                            vfree[bi] = p.c("V")
                            pend.append((t, vb, p.c("V")))

                        def emit_ctx(t2m, vb2, vv2):
                            for c2 in range(2):
                                bank = pd if c2 == 0 else pe2
                                p.op("tensor", lambda e, b=bank, t=t2m, c=c2, vb=vb2: e.matmul(
                                        b[:, 0:DH],
                                        ktm[:, t * DH + c * 128: t * DH + (c + 1) * 128], vb,
                                        start=(t == 0), stop=(t == NTF - 1)),
                                     waits=[("V", max(vv2, kt_v[0]))], inc="T" if c2 == 1 else None)
                            ctx_t[t2m] = p.c("T")

                        for t in range(4):
                            v_mm(t)
                        k_trans(1)
                        kt_v[0] = p.c("V")
                        for t in range(4, NTF):
                            while len(pend) > 2:
                                emit_ctx(*pend.pop(0))
                            v_mm(t)
                        wv_free[0] = p.c("T")
                        while pend:
                            emit_ctx(*pend.pop(0))
                        kev_init = p.c("V")
                        kev = {0: kev_init, 1: kev_init, 2: kev_init}
                        tctx = p.c("T")
                        for c in range(2):
                            bank = pd if c == 0 else pe2
                            p.op("vector", lambda e, b=bank, c=c, h=h: e.tensor_scalar(
                                    ctx[:, (h * 2 + c) * DH:(h * 2 + c + 1) * DH], b[:, 0:DH],
                                    rsk[:, h * 2 + c: h * 2 + c + 1], None, op0=ALU.mult),
                                 waits=[("T", tctx)], inc="V")
                        p.op("tensor", lambda e: None, waits=[("V", p.c("V"))])
                    if debug_stage == "a1":
                        p.op("sync", lambda e: e.dma_start(out=dbg["ctx"][:], in_=ctx[:]),
                             waits=[("V", p.c("V"))], inc="DO", dma=True)
                        p.op("sync", lambda e: None, waits=[("DO", p.c("DO"))])
                        p.flush_block()
                        return nc
                    p.flush_block()

                esA1.close()
                with (
                    nc.sbuf_tensor("wqh", [128, 8 * DH], BF16) as wqh,
                    nc.sbuf_tensor("woh", [128, 2 * 1024], BF16) as woh,
                    nc.sbuf_tensor("qfm", [128, 2 * TOK], BF16) as qfm,
                    nc.sbuf_tensor("attnb", [128, 2 * TOK], BF16) as attnb,
                    nc.sbuf_tensor("eqbf", [128, 2 * DH], F32) as eqbf,
                    nc.psum_tensor("pa_2", [128, 512], F32) as pa,
                    nc.psum_tensor("pb_2", [128, 512], F32) as pb,
                    nc.psum_tensor("pc_2", [128, 512], F32) as pc,
                    nc.psum_tensor("pd_2", [128, 512], F32) as pd,
                    nc.psum_tensor("pt1b_2", [128, 1024], BF16) as pt1b,
                ):
                    bo_done = dma("sync", pvc[:], bo_t[:], "DX")
                    for h in range(HEADS):
                        wq_done = dma("gpsimd", wqh[:], wq_h[h], "DW",
                                      waits=[("T", p.c("T"))])
                        wo_done = dma("gpsimd", woh[:], wo_img[:, 2 * h * 1024:(2 * h + 2) * 1024], "DW")
                        q_evac = [0, 0]
                        q_soft = [0, 0]
                        pend = []

                        def emit_qtrans(t, par, vq):
                            for j in range(2):
                                p.op("tensor", lambda e, j=j, par=par: e.transpose(
                                        pt1b[:, par * 256 + j * 128: par * 256 + (j + 1) * 128],
                                        tbf1[:, par * DH + j * 128: par * DH + (j + 1) * 128],
                                        ident_bf[:]),
                                     waits=[("V", vq)], inc="T" if j == 1 else None)
                            tq2 = p.c("T")
                            p.op("vector", lambda e, t=t, par=par: e.tensor_copy(
                                    bass.AP(qfm, t * 128, [[2 * TOK, 128], [TOK, 2], [1, 128]]),
                                    pt1b[:, par * 256:par * 256 + 256]),
                                 waits=[("T", tq2)], inc="V")
                            q_evac[par] = p.c("V")
                        for t in range(NT):
                            par = t % 2
                            qb = pa if par == 0 else pd
                            cb = 8 if par == 0 else 20
                            for kc in range(8):
                                st = h_norm[:, kc * N + t * 128: kc * N + (t + 1) * 128]
                                mv = wqh[:, kc * DH:(kc + 1) * DH]
                                p.op("tensor", lambda e, qb=qb, s=st, m=mv, k=kc: e.matmul(
                                        qb[:, 0:DH], s, m, start=(k == 0), stop=(k == 7)),
                                     waits=([("DW", wq_done), ("S", q_soft[par]),
                                             ("V", q_evac[par])] if kc == 0 else ()),
                                     inc="T" if kc == 7 else None)
                            tq = p.c("T")
                            p.op("vector", lambda e, qb=qb, cb=cb: e.reduce_max(
                                    cols[:, cb:cb + 1], qb[:, 0:DH], axis=AX.X, negate=True),
                                 waits=[("T", tq)], inc="V")
                            p.op("vector", lambda e, cb=cb: e.tensor_scalar(
                                    cols[:, cb + 1:cb + 2], cols[:, cb:cb + 1], SCALE_QK,
                                    None, op0=ALU.mult),
                                 waits=[("V", p.c("V"))], inc="V")
                            p.op("scalar", lambda e, qb=qb, cb=cb, par=par: e.activation(
                                    eqbf[:, par * DH:(par + 1) * DH], qb[:, 0:DH], AF.Exp,
                                    bias=cols[:, cb + 1:cb + 2],
                                    scale=SCALE_QK, accum_out=cols[:, cb + 2:cb + 3]),
                                 waits=[("V", p.c("V"))], inc="S")
                            p.op("vector", lambda e, cb=cb: e.reciprocal(
                                    cols[:, cb + 3:cb + 4], cols[:, cb + 2:cb + 3]),
                                 waits=[("S", p.c("S"))], inc="V")
                            p.op("vector", lambda e, cb=cb, par=par: e.tensor_scalar(
                                    tbf1[:, par * DH:(par + 1) * DH],
                                    eqbf[:, par * DH:(par + 1) * DH],
                                    cols[:, cb + 3:cb + 4], None, op0=ALU.mult),
                                 waits=[("V", p.c("V")), ("T", p.c("T"))], inc="V")
                            q_soft[par] = p.c("S")
                            pend.append((t, par, p.c("V")))
                            if len(pend) > 1:
                                emit_qtrans(*pend.pop(0))
                        while pend:
                            emit_qtrans(*pend.pop(0))
                        vqall = p.c("V")
                        for ec in range(2):
                            for nb in range(4):
                                for c in range(2):
                                    st = ctx[:, (h * 2 + c) * DH + ec * 128:
                                             (h * 2 + c) * DH + (ec + 1) * 128]
                                    mv = qfm[:, c * TOK + nb * 512: c * TOK + (nb + 1) * 512]
                                    p.op("tensor", lambda e, s=st, m=mv, c=c: e.matmul(
                                            pb[:], s, m, start=(c == 0), stop=(c == 1)),
                                         waits=([("V", vqall if (ec == 0 and nb == 0) else p.c("V"))]
                                                if c == 0 else ()),
                                         inc="T" if c == 1 else None)
                                ta = p.c("T")
                                p.op("vector", lambda e, ec=ec, nb=nb: e.tensor_copy(
                                        attnb[:, ec * TOK + nb * 512: ec * TOK + (nb + 1) * 512],
                                        pb[:]),
                                     waits=[("T", ta)], inc="V")
                        vat = p.c("V")
                        for t in range(NT):
                            for half in range(2):
                                bank = pc if half == 0 else pd
                                for ec in range(2):
                                    st = attnb[:, ec * TOK + t * 128: ec * TOK + (t + 1) * 128]
                                    mv = woh[:, ec * 1024 + half * 512: ec * 1024 + (half + 1) * 512]
                                    p.op("tensor", lambda e, b=bank, s=st, m=mv, ec=ec: e.matmul(
                                            b[:], s, m, start=(ec == 0), stop=(ec == 1)),
                                         waits=(([("DW", wo_done), ("V", vat)]
                                                 if (t == 0 and half == 0) else [("V", p.c("V"))])
                                                if ec == 0 else ()),
                                         inc="T" if ec == 1 else None)
                                tw = p.c("T")
                                dst = stream[:, t * D + half * 512: t * D + (half + 1) * 512]
                                p.op("vector", lambda e, b=bank, d=dst: e.scalar_tensor_tensor(
                                        d, b[:], 1.0, d, op0=ALU.mult, op1=ALU.add),
                                     waits=[("T", tw)], inc="V")
                        p.op("tensor", lambda e: None, waits=[("V", p.c("V"))])
                    for t in range(NT):
                        p.op("vector", lambda e, t=t: e.tensor_add(
                                stream[:, t * D:(t + 1) * D], stream[:, t * D:(t + 1) * D], pvc[:]),
                             waits=([("DX", bo_done), ("V", p.c("V"))] if t == 0 else ()), inc="V")
                    if debug_stage == "a2":
                        p.op("sync", lambda e: e.dma_start(out=dbg["stream"][:], in_=stream[:]),
                             waits=[("V", p.c("V"))], inc="DO", dma=True)
                        p.op("sync", lambda e: None, waits=[("DO", p.c("DO"))])
                        p.flush_block()
                        return nc
                    p.flush_block()

        esA0.close()
        with (
            nc.sbuf_tensor("h2fm", [128, 8 * TOK], BF16) as h2fm,
            nc.sbuf_tensor("gfm", [128, 32 * TB], BF16) as gfm,
            nc.sbuf_tensor("wff1b", [128, 2 * 8 * 512], BF16) as wff1b,
            nc.sbuf_tensor("wff2b", [128, 2 * 4 * 512], BF16) as wff2b,
            nc.psum_tensor("pa2", [128, 512], F32) as pa2,
            nc.psum_tensor("pb2", [128, 512], F32) as pb2,
            nc.psum_tensor("pf0", [128, 512], F32) as pf0,
            nc.psum_tensor("pf1", [128, 512], F32) as pf1,
            nc.psum_tensor("pf2", [128, 512], F32) as pf2,
            nc.psum_tensor("pf3", [128, 512], F32) as pf3,
            nc.psum_tensor("pt2", [128, 1024], BF16) as pt2,
        ):
            pfb = (pf0, pf1, pf2, pf3)
            g1_done = dma("sync", pva[:], ln2g_t[:], "DX")
            g2_done = dma("sync", pvb[:], ln2b_t[:], "DX")
            bf2_done = dma("sync", pvc[:], bff2_t[:], "DX")
            h2v = {}

            def b_ln2(t):
                xt = stream[:, t * D:(t + 1) * D]
                ln_stats(xt, EPS_LN, [("DX", g2_done)])
                p.op("vector", lambda e, xt=xt: e.tensor_scalar(
                        tmp1, xt, cols[:, 0:1], cols[:, 2:3], op0=ALU.subtract, op1=ALU.mult),
                     waits=[("V", p.c("V")), ("S", p.c("S"))], inc="V")
                p.op("vector", lambda e: e.tensor_mul(tmp1, tmp1, pva[:]),
                     waits=[("V", p.c("V"))], inc="V")
                p.op("vector", lambda e: e.tensor_add(tmp1, tmp1, pvb[:]),
                     waits=[("V", p.c("V"))], inc="V")
                cast_transpose_store(tmp1, h2fm, t * 128, TOK, pt2,
                                     extra_waits=[("V", p.c("V"))])
                h2v[t] = p.c("V")

            for t in range(4):
                b_ln2(t)
            g1hist = []
            g2hist = []
            for tb in range(NTB):
                for gg in range(8):
                    buf1 = wff1b[:, (gg % 2) * 8 * 512:(gg % 2 + 1) * 8 * 512]
                    ld1 = dma("sync", buf1,
                              bass.AP(wff1bf, gg * 512, [[8 * FF, 128], [FF, 8], [1, 512]]),
                              "DS", waits=([("T", g1hist[-2])] if len(g1hist) >= 2 else []) +
                                    [("DC", p.c("DC"))])
                    for gj in range(4):
                        g = gg * 4 + gj
                        bank = pa2 if gj % 2 == 0 else pb2
                        for kc in range(8):
                            st = buf1[:, kc * 512 + gj * 128: kc * 512 + (gj + 1) * 128]
                            mv = h2fm[:, kc * TOK + tb * TB: kc * TOK + (tb + 1) * TB]
                            p.op("tensor", lambda e, b=bank, s=st, m=mv, k=kc: e.matmul(
                                    b[:], s, m, start=(k == 0), stop=(k == 7)),
                                 waits=([("DS", ld1), ("S", p.c("S")),
                                         ("V", h2v[4 * tb + 3])] if kc == 0 else ()),
                                 inc="T" if kc == 7 else None)
                        tg = p.c("T")
                        p.op("scalar", lambda e, b=bank, g=g: e.activation(
                                gfm[:, g * 512:(g + 1) * 512], b[:], AF.Gelu_apprx_tanh,
                                bias=bff1_fm[:, g:g + 1]),
                             waits=[("T", tg)], inc="S")
                    g1hist.append(p.c("T"))
                    t_next = 4 * tb + 4 + gg
                    if gg < 4 and t_next < NT:
                        b_ln2(t_next)
                sgelu = p.c("S")
                if debug_stage == "b1" and tb == 0:
                    p.op("sync", lambda e: e.dma_start(out=dbg["hnorm"][:, 0:8 * TOK], in_=h2fm[:]),
                         waits=[("S", sgelu)], inc="DO", dma=True)
                    p.op("sync", lambda e: e.dma_start(out=dbg["g"][:], in_=gfm[:]),
                         waits=[("S", sgelu)], inc="DO", dma=True)
                    p.op("sync", lambda e: None, waits=[("DO", p.c("DO"))])
                    p.flush_block()
                    return nc
                for half in range(2):
                    for gc in range(32):
                        if gc % 4 == 0:
                            prw = (gc // 4) % 2
                            buf2 = wff2b[:, prw * 4 * 512:(prw + 1) * 4 * 512]
                            ld2 = dma("scalar", buf2,
                                      bass.AP(wff2bf, gc * 1024 + half * 512,
                                              [[32 * 1024, 128], [1024, 4], [1, 512]]),
                                      "DT",
                                      waits=([("T", g2hist[-2])] if len(g2hist) >= 2 else []) +
                                            [("DC", p.c("DC"))])
                        prw = (gc // 4) % 2
                        for tt in range(4):
                            st = gfm[:, gc * 512 + tt * 128: gc * 512 + (tt + 1) * 128]
                            mv = wff2b[:, prw * 4 * 512 + (gc % 4) * 512:
                                       prw * 4 * 512 + (gc % 4 + 1) * 512]
                            p.op("tensor", lambda e, b=pfb[tt], s=st, m=mv, g=gc: e.matmul(
                                    b[:], s, m, start=(g == 0), stop=(g == 31)),
                                 waits=([("DT", ld2), ("S", sgelu), ("V", p.c("V"))]
                                        if (gc % 4 == 0 and tt == 0) else ()),
                                 inc="T" if (gc == 31 or (gc % 4 == 3 and tt == 3)) else None)
                        if gc % 4 == 3:
                            g2hist.append(p.c("T"))
                    tf = p.c("T")
                    for tt in range(4):
                        t = tb * 4 + tt
                        dst = stream[:, t * D + half * 512: t * D + (half + 1) * 512]
                        p.op("vector", lambda e, b=pfb[tt], d=dst: e.scalar_tensor_tensor(
                                d, b[:], 1.0, d, op0=ALU.mult, op1=ALU.add),
                             waits=[("T", tf), ("DX", bf2_done)], inc="V")
                        p.op("vector", lambda e, d=dst, h2=half: e.tensor_add(
                                d, d, pvc[:, h2 * 512:(h2 + 1) * 512]),
                             waits=[("V", p.c("V"))], inc="V")
                    p.op("tensor", lambda e: None, waits=[("V", p.c("V"))])
            if debug_stage == "b":
                p.op("sync", lambda e: e.dma_start(out=dbg["stream"][:], in_=stream[:]),
                     waits=[("V", p.c("V"))], inc="DO", dma=True)
                p.op("sync", lambda e: None, waits=[("DO", p.c("DO"))])
                p.flush_block()
                return nc
            p.flush_block()

        with (
            nc.sbuf_tensor("wdb", [128, 8 * 1024], BF16) as wdb,
            nc.sbuf_tensor("x2fm", [128, 2 * 1024], BF16) as x2fm,
            nc.sbuf_tensor("xo", [128, 2 * D], F32) as xo,
            nc.psum_tensor("pa3", [128, 512], F32) as pa3,
            nc.psum_tensor("pb3", [128, 512], F32) as pb3,
            nc.psum_tensor("pt3", [128, 1024], BF16) as pt3,
        ):
            wd_done = dma("gpsimd", wdb[:], wd_img[:], "DW")
            bd_done = dma("sync", pva[:], bd_t[:], "DX")
            x2v = {}
            xo_done = {}

            def c_front(t):
                xt = stream[:, t * D:(t + 1) * D]
                xob = xo[:, (t % 2) * D:(t % 2 + 1) * D]
                xo_done[t] = dma("sync", xob, x_core[t * 128:(t + 1) * 128, :], "DX",
                                 waits=[("V", p.c("V"))])
                p.op("scalar", lambda e, xt=xt: e.activation(tbf1[:], xt, AF.Copy),
                     waits=[("V", p.c("V")), ("T", p.c("T"))], inc="S")
                sh = p.c("S")
                for j2 in range(8):
                    p.op("tensor", lambda e, j2=j2, t=t: e.transpose(
                            pt3[:, j2 * 128:(j2 + 1) * 128], tbf1[:, j2 * 128:(j2 + 1) * 128],
                            ident_bf[:]),
                         waits=[("S", sh)] if j2 == 0 else (), inc="T" if j2 == 7 else None)
                p.op("vector", lambda e, t=t: e.tensor_copy(
                        x2fm[:, (t % 2) * 1024:(t % 2 + 1) * 1024], pt3[:]),
                     waits=[("T", p.c("T"))], inc="V")
                x2v[t] = p.c("V")

            def c_back(t):
                xt = stream[:, t * D:(t + 1) * D]
                xob = xo[:, (t % 2) * D:(t % 2 + 1) * D]
                for half in range(2):
                    bank = pa3 if half == 0 else pb3
                    for kc in range(8):
                        st = x2fm[:, (t % 2) * 1024 + kc * 128:(t % 2) * 1024 + (kc + 1) * 128]
                        mv = wdb[:, kc * 1024 + half * 512: kc * 1024 + (half + 1) * 512]
                        p.op("tensor", lambda e, b=bank, s=st, m=mv, k=kc: e.matmul(
                                b[:], s, m, start=(k == 0), stop=(k == 7)),
                             waits=([("V", x2v[t]), ("DW", wd_done), ("V", p.c("V"))]
                                    if kc == 0 else ()),
                             inc="T" if kc == 7 else None)
                    tw = p.c("T")
                    p.op("vector", lambda e, b=bank, h2=half: e.tensor_add(
                            tmp1[:, h2 * 512:(h2 + 1) * 512], b[:],
                            pva[:, h2 * 512:(h2 + 1) * 512]),
                         waits=[("T", tw), ("DX", bd_done)], inc="V")
                ln_stats(tmp1, EPS_LN, [("V", p.c("V"))])
                p.op("vector", lambda e: e.tensor_scalar(
                        tmp1, tmp1, cols[:, 0:1], cols[:, 2:3], op0=ALU.subtract, op1=ALU.mult),
                     waits=[("V", p.c("V"))], inc="V")
                p.op("vector", lambda e, xt=xt, xob=xob: e.tensor_add(xt, tmp1, xob),
                     waits=[("V", p.c("V")), ("DX", xo_done[t])], inc="V")

            c_front(0)
            for t in range(NT):
                if t + 1 < NT:
                    c_front(t + 1)
                c_back(t)
            if debug_stage == "c":
                p.op("sync", lambda e: e.dma_start(out=dbg["stream"][:], in_=stream[:]),
                     waits=[("V", p.c("V"))], inc="DO", dma=True)
                p.op("sync", lambda e: None, waits=[("DO", p.c("DO"))])
                p.flush_block()
                return nc
            p.flush_block()

        es_bc.close()
        if True:
            _e = es.enter_context
            tfm = _e(nc.sbuf_tensor("tfm", [128, 8 * TB], BF16))
            gateb = _e(nc.sbuf_tensor("gateb", [128, 64], BF16))
            a1tb = _e(nc.sbuf_tensor("a1tb", [128, 8 * 128], BF16))
            a3tb = _e(nc.sbuf_tensor("a3tb", [128, 8 * 128], BF16))
            b2tab = _e(nc.sbuf_tensor("b2tab", [128, 1024], BF16))
            b2tbb = _e(nc.sbuf_tensor("b2tbb", [128, 1024], BF16))
            actb = _e(nc.sbuf_tensor("actb", [128, 2 * 5120], BF16))
            w2b = _e(nc.sbuf_tensor("w2b", [128, 2 * 4096], BF16))
            r2accA = _e(nc.sbuf_tensor("r2accA", [128, TB], F32))
            r2accB = _e(nc.sbuf_tensor("r2accB", [128, TB], F32))
            r1m = _e(nc.sbuf_tensor("r1mS", [128, 8 * TB], BF16))
            r3m = _e(nc.sbuf_tensor("r3mS", [128, 8 * TB], BF16))
            bsb1 = _e(nc.sbuf_tensor("bsb1", [128, 2 * TB], BF16))
            bsb3 = _e(nc.sbuf_tensor("bsb3", [128, 2 * TB], BF16))
            sbuf_s = _e(nc.sbuf_tensor("sbuf_s", [128, 2 * TB], BF16))
            mbuf = _e(nc.sbuf_tensor("mbuf", [128, 2 * TB], BF16))
            t2b = _e(nc.sbuf_tensor("t2b", [128, 2 * TB], BF16))
            wbc = _e(nc.sbuf_tensor("wbc", [128, 8 * TB], BF16))
            wtmp = _e(nc.sbuf_tensor("wtmp", [1, TB], BF16))
            wrall = _e(nc.sbuf_tensor("wrall", [128, 32], F32))
            r2wA = _e(nc.sbuf_tensor("r2wA", [128, TB], BF16))
            r2wB = _e(nc.sbuf_tensor("r2wB", [128, TB], BF16))
            actw = _e(nc.sbuf_tensor("actw", [128, 32 * 512], BF16))
            probs = _e(nc.sbuf_tensor("probs", [128, 8], F32))
            s8 = _e(nc.sbuf_tensor("s8", [128, 8], F32))
            i8f = _e(nc.sbuf_tensor("i8f", [128, 8], F32))
            wrow = _e(nc.sbuf_tensor("wrow", [128, 8], F32))
            pB1 = _e(nc.psum_tensor("pB1", [128, 512], F32))
            pH1a = _e(nc.psum_tensor("pH1a", [128, 512], F32))
            pH1b = _e(nc.psum_tensor("pH1b", [128, 512], F32))
            pH3a = _e(nc.psum_tensor("pH3a", [128, 512], F32))
            pH3b = _e(nc.psum_tensor("pH3b", [128, 512], F32))
            pAW = _e(nc.psum_tensor("pAW", [128, 512], F32))
            pR2A = _e(nc.psum_tensor("pR2A", [128, 512], F32))
            pR2B = _e(nc.psum_tensor("pR2B", [128, 512], F32))
        if True:
            nw_done = dma("sync", pva[:], nw_t[:], "DX")
            gate_done = dma("gpsimd", gateb[:], gate_img[:], "DW")
            a1_done = dma("gpsimd", a1tb[:], a1t_img[:], "DW")
            a3_done = dma("gpsimd", a3tb[:], a3t_img[:], "DW")
            b2a_done = dma("gpsimd", b2tab[:], b2ta_img[:], "DW")
            b2b_done = dma("gpsimd", b2tbb[:], b2tb_img[:], "DW")
            dhist = []
            for tb in range(NTB):
                for tt in range(4):
                    t = tb * 4 + tt
                    yt = stream[:, t * D:(t + 1) * D]
                    ln_stats(yt, EPS_LN, [("DX", nw_done)])
                    p.op("vector", lambda e, yt=yt: e.tensor_scalar(
                            tmp1, yt, cols[:, 0:1], cols[:, 2:3],
                            op0=ALU.subtract, op1=ALU.mult),
                         waits=[("V", p.c("V")), ("S", p.c("S"))], inc="V")
                    rms_stats(tmp1, EPS_RMS, [("V", p.c("V"))])
                    p.op("vector", lambda e: e.tensor_scalar(
                            tmp1, tmp1, cols[:, 2:3], None, op0=ALU.mult),
                         waits=[("V", p.c("V")), ("S", p.c("S"))], inc="V")
                    p.op("vector", lambda e: e.tensor_mul(tmp1, tmp1, pva[:]),
                         waits=[("V", p.c("V")), ("T", p.c("T"))], inc="V")
                    vt = p.c("V")
                    for j in range(8):
                        bank = pH1a if j < 4 else pH1b
                        p.op("tensor", lambda e, j=j, bank=bank: e.transpose(
                                bank[:, (j % 4) * 128:(j % 4 + 1) * 128],
                                tmp1[:, j * 128:(j + 1) * 128], ident_f[:]),
                             waits=[("V", vt)] if j == 0 else (),
                             inc="T" if j % 4 == 3 else None)
                    tpose = p.c("T")
                    for half2 in range(2):
                        bank = pH1a if half2 == 0 else pH1b
                        p.op("vector", lambda e, b=bank, h3=half2, tt=tt: e.tensor_copy(
                                bass.AP(tfm, (h3 * 4) * TB + tt * 128,
                                        [[8 * TB, 128], [TB, 4], [1, 128]]), b[:]),
                             waits=[("T", tpose)], inc="V")
                vtfm = p.c("V")
                for tt in range(4):
                    for kc in range(8):
                        st = tfm[:, kc * TB + tt * 128: kc * TB + (tt + 1) * 128]
                        mv = gateb[:, kc * 8:(kc + 1) * 8]
                        p.op("tensor", lambda e, s=st, m=mv, k=kc: e.matmul(
                                pB1[:, 0:8], s, m, start=(k == 0), stop=(k == 7)),
                             waits=([("DW", gate_done), ("V", p.c("V")), ("S", p.c("S"))]
                                    if kc == 0 else ()),
                             inc="T" if kc == 7 else None)
                    tg = p.c("T")
                    p.op("vector", lambda e: e.reduce_max(
                            cols[:, 8:9], pB1[:, 0:8], axis=AX.X, negate=True),
                         waits=[("T", tg)], inc="V")
                    p.op("scalar", lambda e: e.activation(
                            probs[:], pB1[:, 0:8], AF.Exp, bias=cols[:, 8:9],
                            accum_out=cols[:, 10:11]),
                         waits=[("V", p.c("V"))], inc="S")
                    p.op("vector", lambda e: e.reciprocal(cols[:, 11:12], cols[:, 10:11]),
                         waits=[("S", p.c("S"))], inc="V")
                    p.op("vector", lambda e: e.tensor_scalar(
                            probs[:], probs[:], cols[:, 11:12], None, op0=ALU.mult),
                         waits=[("V", p.c("V"))], inc="V")
                    p.op("vector", lambda e: e.reduce_max(cols[:, 12:13], probs[:], axis=AX.X),
                         waits=[("V", p.c("V"))], inc="V")
                    p.op("vector", lambda e: e.tensor_scalar(
                            s8[:], probs[:], cols[:, 12:13], None, op0=ALU.is_equal),
                         waits=[("V", p.c("V"))], inc="V")
                    p.op("vector", lambda e: e.tensor_mul(s8[:], s8[:], nine_m[:]),
                         waits=[("V", p.c("V"))], inc="V")
                    p.op("vector", lambda e: e.tensor_scalar(
                            s8[:], s8[:], -1.0, 9.0, op0=ALU.mult, op1=ALU.add),
                         waits=[("V", p.c("V"))], inc="V")
                    p.op("vector", lambda e: e.tensor_reduce(
                            cols[:, 16:17], s8[:], axis=AX.X, op=ALU.min),
                         waits=[("V", p.c("V"))], inc="V")
                    p.op("vector", lambda e: e.tensor_scalar(
                            s8[:], iota8[:], cols[:, 16:17], None, op0=ALU.is_equal),
                         waits=[("V", p.c("V"))], inc="V")
                    p.op("vector", lambda e: e.tensor_mul(i8f[:], probs[:], s8[:]),
                         waits=[("V", p.c("V"))], inc="V")
                    p.op("vector", lambda e: e.tensor_sub(i8f[:], probs[:], i8f[:]),
                         waits=[("V", p.c("V"))], inc="V")
                    p.op("vector", lambda e: e.reduce_max(cols[:, 13:14], i8f[:], axis=AX.X),
                         waits=[("V", p.c("V"))], inc="V")
                    p.op("vector", lambda e: e.tensor_scalar(
                            i8f[:], i8f[:], cols[:, 13:14], None, op0=ALU.is_equal),
                         waits=[("V", p.c("V"))], inc="V")
                    p.op("vector", lambda e: e.tensor_mul(i8f[:], i8f[:], nine_m[:]),
                         waits=[("V", p.c("V"))], inc="V")
                    p.op("vector", lambda e: e.tensor_scalar(
                            i8f[:], i8f[:], -1.0, 9.0, op0=ALU.mult, op1=ALU.add),
                         waits=[("V", p.c("V"))], inc="V")
                    p.op("vector", lambda e: e.tensor_reduce(
                            cols[:, 17:18], i8f[:], axis=AX.X, op=ALU.min),
                         waits=[("V", p.c("V"))], inc="V")
                    p.op("vector", lambda e: e.tensor_scalar(
                            i8f[:], iota8[:], cols[:, 17:18], None, op0=ALU.is_equal),
                         waits=[("V", p.c("V"))], inc="V")
                    p.op("vector", lambda e: e.tensor_add(cols[:, 14:15], cols[:, 12:13], cols[:, 13:14]),
                         waits=[("V", p.c("V"))], inc="V")
                    p.op("vector", lambda e: e.reciprocal(cols[:, 15:16], cols[:, 14:15]),
                         waits=[("V", p.c("V"))], inc="V")
                    p.op("vector", lambda e: e.tensor_scalar(
                            cols[:, 18:19], cols[:, 12:13], cols[:, 15:16], None, op0=ALU.mult),
                         waits=[("V", p.c("V"))], inc="V")
                    p.op("vector", lambda e: e.tensor_scalar(
                            cols[:, 19:20], cols[:, 13:14], cols[:, 15:16], None, op0=ALU.mult),
                         waits=[("V", p.c("V"))], inc="V")
                    p.op("vector", lambda e: e.tensor_scalar(
                            s8[:], s8[:], cols[:, 18:19], None, op0=ALU.mult),
                         waits=[("V", p.c("V"))], inc="V")
                    p.op("vector", lambda e: e.scalar_tensor_tensor(
                            probs[:], i8f[:], cols[:, 19:20], s8[:], op0=ALU.mult, op1=ALU.add),
                         waits=[("V", p.c("V"))], inc="V")
                    p.op("vector", lambda e, tt=tt: e.tensor_copy(
                            wrall[:, tt * 8:(tt + 1) * 8], probs[:]),
                         waits=[("V", p.c("V"))], inc="V")
                vw = p.c("V")
                for ex in range(8):
                    for tt in range(4):
                        p.op("tensor", lambda e, ex=ex, tt=tt: e.transpose(
                                pH3a[0:1, tt * 128:(tt + 1) * 128],
                                wrall[:, tt * 8 + ex: tt * 8 + ex + 1], ident_f[:]),
                             waits=[("V", vw)] if (ex == 0 and tt == 0) else (),
                             inc="T" if tt == 3 else None)
                    tx = p.c("T")
                    p.op("vector", lambda e: e.tensor_copy(wtmp[0:1, :], pH3a[0:1, 0:512]),
                         waits=[("T", tx)], inc="V")
                    vtx = p.c("V")
                    p.op("tensor", lambda e: e.matmul(
                            pB1[:], ones_bf[:], wtmp[0:1, :], start=True, stop=True),
                         waits=[("V", vtx)], inc="T")
                    tbx = p.c("T")
                    p.op("vector", lambda e, ex=ex: e.tensor_copy(
                            wbc[:, ex * TB:(ex + 1) * TB], pB1[:]),
                         waits=[("T", tbx)], inc="V")
                    p.op("tensor", lambda e: None, waits=[("V", p.c("V"))])
                for (img, dwv, buf) in ((a1tb, a1_done, r1m), (a3tb, a3_done, r3m)):
                    for kc in range(8):
                        st = img[:, kc * 128:(kc + 1) * 128]
                        mv = tfm[:, kc * TB:(kc + 1) * TB]
                        p.op("tensor", lambda e, s=st, m=mv, k=kc: e.matmul(
                                pB1[:], s, m, start=(k == 0), stop=(k == 7)),
                             waits=([("DW", dwv), ("V", p.c("V")), ("S", p.c("S"))]
                                    if kc == 0 else ()),
                             inc="T" if kc == 7 else None)
                    tr = p.c("T")
                    for ex in range(8):
                        p.op("vector", lambda e, ex=ex, b=buf: e.tensor_scalar(
                                b[:, ex * TB:(ex + 1) * TB], pB1[:],
                                emask2[:, ex:ex + 1], None, op0=ALU.mult),
                             waits=[("T", tr)], inc="V")
                    p.op("tensor", lambda e: None, waits=[("V", p.c("V"))])
                p.op("vector", lambda e: e.memset(r2wA[:], 0.0),
                     waits=[("V", p.c("V")), ("T", p.c("T"))], inc="V")
                p.op("vector", lambda e: e.memset(r2wB[:], 0.0), waits=(), inc="V")
                p.op("vector", lambda e: e.memset(r2accA[:], 0.0), waits=(), inc="V")
                p.op("vector", lambda e: e.memset(r2accB[:], 0.0), waits=(), inc="V")
                r2add_prev = p.c("V")
                silu_h = [p.c("S"), p.c("S")]
                m_h = [p.c("V"), p.c("V")]
                r2_h = [p.c("T"), p.c("T")]
                aw_h = [p.c("T"), p.c("T")]
                aw_evac_prev = p.c("V")
                dstate = {"bank_free": p.c("V"), "grp": {}, "bsb": {}}

                def d_load_group(gg):
                    pr = gg % 2
                    wt = [("T", p.c("T"))] if (tb > 0 or gg >= 2) else []
                    dma("sync", actb[:, pr * 5120:(pr + 1) * 5120],
                        actbf[:, gg * 5120:(gg + 1) * 5120], "DS",
                        waits=wt + [("DC", p.c("DC"))])
                    dstate["grp"][gg] = p.c("DS")

                def d_bases(fc):
                    pr = (fc // 2) % 2
                    fo = fc % 2
                    bp = fc % 2
                    for (wo2, bout) in ((0, bsb1), (2048, bsb3)):
                        for kc in range(8):
                            st = actb[:, pr * 5120 + wo2 + kc * 256 + fo * 128:
                                    pr * 5120 + wo2 + kc * 256 + (fo + 1) * 128]
                            mv = tfm[:, kc * TB:(kc + 1) * TB]
                            p.op("tensor", lambda e, s=st, m=mv, k=kc: e.matmul(
                                    pB1[:], s, m, start=(k == 0), stop=(k == 7)),
                                 waits=([("DS", dstate["grp"][fc // 2]),
                                         ("V", dstate["bank_free"])] if kc == 0 else ()),
                                 inc="T" if kc == 7 else None)
                            del st, mv
                        tmm = p.c("T")
                        p.op("vector", lambda e, o=bout, bp=bp: e.tensor_copy(
                                o[:, bp * TB:(bp + 1) * TB], pB1[:]),
                             waits=[("T", tmm)], inc="V")
                        dstate["bank_free"] = p.c("V")
                    dstate["bsb"][fc] = p.c("V")

                dpend = []

                def emit_r2aw(fc2, ex, sl, pr42, fo2, m_v, t2_v):
                    bank = pR2A if ex < 4 else pR2B
                    off = 32 * (ex % 4)
                    p.op("tensor", lambda e, b=bank, off=off, pr4=pr42, fo=fo2, ex=ex,
                                sl=sl: e.matmul(
                            b[off:off + 32, :],
                            actb[:, pr4 * 5120 + 4608 + fo * 256 + 32 * ex:
                                 pr4 * 5120 + 4608 + fo * 256 + 32 * (ex + 1)],
                            mbuf[:, sl * TB:(sl + 1) * TB],
                            start=True, stop=True, tile_position=(0, off)),
                         waits=[("V", m_v), ("V", r2add_prev)], inc="T")
                    r2_h[1] = r2_h[0]
                    r2_h[0] = p.c("T")
                    p.op("tensor", lambda e, sl=sl, ex=ex: e.matmul(
                            pAW[:], ident_bf[:], t2b[:, sl * TB:(sl + 1) * TB],
                            start=(ex == 0), stop=(ex == 7)),
                         waits=([("V", t2_v), ("V", aw_evac_prev)] if ex == 0
                                else [("V", t2_v)]),
                         inc="T")
                    aw_h[1] = aw_h[0]
                    aw_h[0] = p.c("T")

                d_load_group(0)
                d_load_group(1)
                d_bases(0)
                for fc in range(32):
                    pr4 = (fc // 2) % 2
                    fo = fc % 2
                    bpr = fc % 2
                    for ex in range(8):
                        sl = ex % 2
                        hb = pH1a if sl == 0 else pH1b
                        h3b = pH3a if sl == 0 else pH3b
                        if ex == 1 and fc + 1 < 32:
                            d_bases(fc + 1)
                        if ex == 4 and fc % 2 == 0 and fc >= 2 and fc // 2 + 1 < 16:
                            d_load_group(fc // 2 + 1)
                        p.op("tensor", lambda e, hb=hb, pr4=pr4, fo=fo, ex=ex: e.matmul(
                                hb[:], actb[:, pr4 * 5120 + 4096 + fo * 128:
                                            pr4 * 5120 + 4096 + (fo + 1) * 128],
                                r1m[:, ex * TB:(ex + 1) * TB], start=True, stop=False),
                             waits=[("S", silu_h[1]), ("V", dstate["bsb"][fc])], inc=None)
                        p.op("tensor", lambda e, hb=hb, b2=bpr: e.matmul(
                                hb[:], ident_bf[:], bsb1[:, b2 * TB:(b2 + 1) * TB],
                                start=False, stop=True),
                             waits=(), inc="T")
                        h1d = p.c("T")
                        p.op("scalar", lambda e, sl=sl, hb=hb: e.activation(
                                sbuf_s[:, sl * TB:(sl + 1) * TB], hb[:], AF.Silu),
                             waits=[("T", h1d), ("V", m_h[1])], inc="S")
                        silu_h = [p.c("S"), silu_h[0]]
                        p.op("tensor", lambda e, h3b=h3b, pr4=pr4, fo=fo, ex=ex: e.matmul(
                                h3b[:], actb[:, pr4 * 5120 + 4352 + fo * 128:
                                            pr4 * 5120 + 4352 + (fo + 1) * 128],
                                r3m[:, ex * TB:(ex + 1) * TB], start=True, stop=False),
                             waits=[("V", m_h[1])], inc=None)
                        p.op("tensor", lambda e, h3b=h3b, b2=bpr: e.matmul(
                                h3b[:], ident_bf[:], bsb3[:, b2 * TB:(b2 + 1) * TB],
                                start=False, stop=True),
                             waits=(), inc="T")
                        h3d = p.c("T")
                        p.op("vector", lambda e, sl=sl, h3b=h3b: e.tensor_mul(
                                mbuf[:, sl * TB:(sl + 1) * TB],
                                sbuf_s[:, sl * TB:(sl + 1) * TB], h3b[:]),
                             waits=[("S", silu_h[0]), ("T", h3d), ("T", r2_h[1])],
                             inc="V")
                        m_prev = p.c("V")
                        m_h = [m_prev, m_h[0]]
                        p.op("vector", lambda e, sl=sl, ex=ex: e.tensor_mul(
                                t2b[:, sl * TB:(sl + 1) * TB],
                                mbuf[:, sl * TB:(sl + 1) * TB], wbc[:, ex * TB:(ex + 1) * TB]),
                             waits=[("T", aw_h[1])], inc="V")
                        dpend.append((fc, ex, sl, pr4, fo, m_prev, p.c("V")))
                        if len(dpend) > 1:
                            emit_r2aw(*dpend.pop(0))
                    if dpend:
                        emit_r2aw(*dpend.pop(0))
                    p.op("vector", lambda e: e.tensor_add(r2accA[:], r2accA[:], pR2A[:]),
                         waits=[("T", r2_h[0])], inc="V")
                    p.op("vector", lambda e: e.tensor_add(r2accB[:], r2accB[:], pR2B[:]),
                         waits=(), inc="V")
                    r2add_prev = p.c("V")
                    p.op("vector", lambda e, fc=fc: e.tensor_copy(
                            actw[:, fc * 512:(fc + 1) * 512], pAW[:]),
                         waits=[("T", aw_h[0])], inc="V")
                    aw_evac_prev = p.c("V")
                for ex in range(8):
                    bank = r2accA if ex < 4 else r2accB
                    dst = r2wA if ex < 4 else r2wB
                    off = 32 * (ex % 4)
                    p.op("vector", lambda e, b=bank, d=dst, off=off, ex=ex: e.scalar_tensor_tensor(
                            d[off:off + 16, :], b[off:off + 16, :], 2.0,
                            wbc[off:off + 16, ex * TB:(ex + 1) * TB],
                            op0=ALU.mult, op1=ALU.mult),
                         waits=[("V", r2add_prev)], inc="V")
                r2w_ready = p.c("V")
                aw_all = p.c("V")
                if debug_stage == "d1" and tb == 0:
                    p.op("sync", lambda e: e.dma_start(out=dbg["hnorm"][:, 0:4096], in_=tfm[:]),
                         waits=[("V", aw_all)], inc="DO", dma=True)
                    p.op("sync", lambda e: e.dma_start(out=dbg["hnorm"][:, 4096:8192], in_=wbc[:]),
                         waits=[("V", aw_all)], inc="DO", dma=True)
                    p.op("sync", lambda e: e.dma_start(out=dbg["hnorm"][:, 8192:8704], in_=r2wA[:]),
                         waits=[("V", aw_all)], inc="DO", dma=True)
                    p.op("sync", lambda e: e.dma_start(out=dbg["hnorm"][:, 8704:9216], in_=r2wB[:]),
                         waits=[("V", aw_all)], inc="DO", dma=True)
                    p.op("sync", lambda e: e.dma_start(out=dbg["g"][:], in_=actw[:]),
                         waits=[("V", aw_all)], inc="DO", dma=True)
                    p.op("sync", lambda e: None, waits=[("DO", p.c("DO"))])
                    p.flush_block()
                    return nc
                dbanks = (pB1, pH1a, pH1b, pH3a, pH3b, pAW, pR2A, pR2B)
                w2rdy = {}
                w2last = {}

                def d_load_w2(g4):
                    prw = g4 % 2
                    wt = ([("T", w2last[g4 - 2])] if (g4 - 2) in w2last
                          else ([("T", p.c("T"))] if tb > 0 else []))
                    dma("scalar", w2b[:, prw * 4096:(prw + 1) * 4096],
                        w2bf[:, g4 * 4096:(g4 + 1) * 4096], "DT",
                        waits=wt + [("DC", p.c("DC"))])
                    w2rdy[g4] = p.c("DT")

                d_load_w2(0)
                d_load_w2(1)
                for fc in range(32):
                    g2 = fc // 4
                    if fc % 4 == 0 and g2 >= 1 and g2 + 1 < 8:
                        d_load_w2(g2 + 1)
                    prw = g2 % 2
                    wcol = w2b[:, prw * 4096 + (fc % 4) * 1024: prw * 4096 + (fc % 4 + 1) * 1024]
                    for tt in range(4):
                        for half in range(2):
                            bank = dbanks[tt * 2 + half]
                            st = actw[:, fc * 512 + tt * 128: fc * 512 + (tt + 1) * 128]
                            mv = wcol[:, half * 512:(half + 1) * 512]
                            p.op("tensor", lambda e, b=bank, s=st, m=mv, fc=fc: e.matmul(
                                    b[:], s, m, start=(fc == 0), stop=False),
                                 waits=([("DT", w2rdy[g2]), ("V", aw_all), ("S", p.c("S"))]
                                        if (tt == 0 and half == 0) else ()),
                                 inc="T" if (tt == 3 and half == 1) else None)
                    w2last[g2] = p.c("T")
                for tt in range(4):
                    t = tb * 4 + tt
                    for half in range(2):
                        bank = dbanks[tt * 2 + half]
                        p.op("tensor", lambda e, b=bank, tt=tt, h2=half: e.matmul(
                                b[:], r2wA[:, tt * 128:(tt + 1) * 128],
                                b2tab[:, h2 * 512:(h2 + 1) * 512], start=False, stop=False),
                             waits=([("V", r2w_ready), ("DW", b2a_done)]
                                    if (tt == 0 and half == 0) else ()),
                             inc=None)
                        p.op("tensor", lambda e, b=bank, tt=tt, h2=half: e.matmul(
                                b[:], r2wB[:, tt * 128:(tt + 1) * 128],
                                b2tbb[:, h2 * 512:(h2 + 1) * 512], start=False, stop=True),
                             waits=([("DW", b2b_done)] if (tt == 0 and half == 0) else ()),
                             inc="T")
                    tdn = p.c("T")
                    for half in range(2):
                        bank = dbanks[tt * 2 + half]
                        p.op("vector", lambda e, b=bank, h2=half: e.tensor_copy(
                                tmp1[:, h2 * 512:(h2 + 1) * 512], b[:]),
                             waits=[("T", tdn), ("S", p.c("S")), ("V", p.c("V"))], inc="V")
                    ln_stats(tmp1, EPS_LN, [("V", p.c("V"))])
                    p.op("vector", lambda e: e.tensor_scalar(
                            tmp1, tmp1, cols[:, 0:1], cols[:, 2:3],
                            op0=ALU.subtract, op1=ALU.mult),
                         waits=[("V", p.c("V"))], inc="V")
                    yt = stream[:, t * D:(t + 1) * D]
                    p.op("vector", lambda e, yt=yt: e.tensor_add(yt, tmp1, yt),
                         waits=[("V", p.c("V"))], inc="V")
                    p.op("sync", lambda e, t=t, yt=yt: e.dma_start(
                            out=y_out[t * 128:(t + 1) * 128, :], in_=yt),
                         waits=[("V", p.c("V"))], inc="DO", dma=True)
            p.op("sync", lambda e: None, waits=[("DO", p.c("DO"))])
            p.flush_block()

    return nc



def _prep_inputs(inputs):
    g = {k: np.asarray(v) for k, v in inputs.items()}
    out = {}

    def img8(w):
        M = w.shape[1]
        return np.ascontiguousarray(w.reshape(8, 128, M).transpose(1, 0, 2).reshape(128, 8 * M))

    def img32(w):
        M = w.shape[1]
        return np.ascontiguousarray(w.reshape(32, 128, M).transpose(1, 0, 2).reshape(128, 32 * M))

    for name, w in (("wq_h", g["L0_attn_wq"]), ("wk_h", g["L0_attn_wk"]), ("wv_h", g["L0_attn_wv"])):
        out[name] = np.ascontiguousarray(
            w.reshape(8, 128, 4, 256).transpose(2, 1, 0, 3).reshape(4, 128, 8 * 256))
    out["wo_img"] = img8(g["L0_attn_wo"])
    out["wff1_img"] = img8(g["L0_attn_w_ff1"])
    out["wff2_img"] = img32(g["L0_attn_w_ff2"])
    out["wd_img"] = img8(g["L0_attn_w_d"])
    out["w1t_img"] = img8(np.ascontiguousarray(g["L0_ffn_w1"].T))
    out["w3t_img"] = img8(np.ascontiguousarray(g["L0_ffn_w3"].T))
    out["w2t_img"] = img32(np.ascontiguousarray(g["L0_ffn_w2"].T))
    out["gate_img"] = img8(np.ascontiguousarray(g["L0_ffn_gate_w"].T))
    out["a1t_img"] = img8(np.ascontiguousarray(
        g["L0_ffn_a1"].transpose(2, 0, 1).reshape(1024, 128)))
    out["a3t_img"] = img8(np.ascontiguousarray(
        g["L0_ffn_a3"].transpose(2, 0, 1).reshape(1024, 128)))
    out["b1t_img"] = np.ascontiguousarray(g["L0_ffn_b1"].transpose(0, 2, 1).reshape(128, 4096))
    out["b3t_img"] = np.ascontiguousarray(g["L0_ffn_b3"].transpose(0, 2, 1).reshape(128, 4096))
    a2p = np.zeros((4096, 256), np.float32)
    for e in range(8):
        a2p[:, 32 * e:32 * e + 16] = g["L0_ffn_a2"][e].T
    out["a2t_img"] = img32(a2p)
    blocks = []
    for gg in range(16):
        blocks.extend([out["w1t_img"][:, kc * FF + gg * 256: kc * FF + (gg + 1) * 256]
                       for kc in range(8)])
        blocks.extend([out["w3t_img"][:, kc * FF + gg * 256: kc * FF + (gg + 1) * 256]
                       for kc in range(8)])
        blocks.append(out["b1t_img"][:, gg * 256:(gg + 1) * 256])
        blocks.append(out["b3t_img"][:, gg * 256:(gg + 1) * 256])
        blocks.append(out["a2t_img"][:, gg * 512:(gg + 1) * 512])
    out["actstream"] = np.ascontiguousarray(np.concatenate(blocks, axis=1))
    b2t = g["L0_ffn_b2"].transpose(0, 2, 1)
    b2a = np.zeros((128, 1024), np.float32)
    b2b = np.zeros((128, 1024), np.float32)
    for e in range(4):
        b2a[32 * e:32 * e + 16] = b2t[e]
        b2b[32 * e:32 * e + 16] = b2t[e + 4]
    out["b2ta_img"] = b2a
    out["b2tb_img"] = b2b
    tile = lambda v: np.ascontiguousarray(np.tile(np.asarray(v, np.float32)[None, :], (128, 1)))
    out["ln1g_t"] = tile(g["L0_attn_ln1_g"]); out["ln1b_t"] = tile(g["L0_attn_ln1_b"])
    out["ln2g_t"] = tile(g["L0_attn_ln2_g"]); out["ln2b_t"] = tile(g["L0_attn_ln2_b"])
    out["bo_t"] = tile(g["L0_attn_bo"]); out["bff2_t"] = tile(g["L0_attn_b_ff2"])
    out["bd_t"] = tile(g["L0_attn_b_d"]); out["nw_t"] = tile(g["L0_ffn_norm_w"])
    out["bff1_fm"] = np.ascontiguousarray(g["L0_attn_b_ff1"].reshape(32, 128).T)
    out["ident_bf"] = np.eye(128, dtype=bf)
    out["ident_f"] = np.eye(128, dtype=np.float32)
    out["ones_bf"] = np.ones((1, 128), dtype=bf)
    out["iota8"] = np.ascontiguousarray(
        np.tile(np.arange(8, dtype=np.float32)[None, :], (128, 1)))
    out["nine_m"] = np.ascontiguousarray(
        9.0 - np.tile(np.arange(8, dtype=np.float32)[None, :], (128, 1)))
    ep = np.zeros((128, 2), np.float32)
    ep[:, 0] = EPS_LN
    ep[:, 1] = EPS_RMS
    out["epsc"] = ep
    em = np.zeros((128, 8), np.float32)
    for e in range(8):
        em[16 * e:16 * e + 16, e] = 2.0
    out["emask2"] = em
    return out


def kernel(**inputs):
    shared = _prep_inputs(inputs)
    x = np.asarray(inputs["L0_attn_x"], np.float32)
    nc = build(DEBUG_STAGE)
    in_maps = []
    for c in range(8):
        b, h = c // 2, c % 2
        xc = np.ascontiguousarray(
            np.concatenate([x[b, h * TOK:(h + 1) * TOK], x[b, (1 - h) * TOK:(2 - h) * TOK]],
                           axis=0))
        m = dict(shared)
        m["x_core"] = xc
        in_maps.append(m)
    res = run_bass_kernel_spmd(nc, in_maps, core_ids=list(range(8)))
    out = np.empty((B, N, D), np.float32)
    for c in range(8):
        b, h = c // 2, c % 2
        out[b, h * TOK:(h + 1) * TOK, :] = res.results[c]["y"]
    return out
